# version 12
# baseline (speedup 1.0000x reference)
import numpy as np
from contextlib import ExitStack

import concourse.bass as bass
import concourse.mybir as mybir
from concourse.bass_utils import run_bass_kernel_spmd

F32 = mybir.dt.float32
BF16 = mybir.dt.bfloat16
I32 = mybir.dt.int32
AF = mybir.ActivationFunctionType
OP = mybir.AluOpType

NCORES = 8
ENGINES = ("sync", "tensor", "vector", "scalar", "gpsimd")

CFG_REAL = dict(D=2048, H=5632, T=1024, TT=512)


class Prog:
    def __init__(self):
        self.ops = {e: [] for e in ENGINES}
        self.cnt = {}
        self.sems = {}

    def bump(self, sem, amt):
        self.cnt[sem] = self.cnt.get(sem, 0) + amt
        return self.cnt[sem]

    def wait(self, eng, sem, val):
        if val:
            self.ops[eng].append(lambda E, s=sem, v=val: E.wait_ge(self.sems[s], v))

    def op(self, eng, fn, inc=None):
        if inc is None:
            self.ops[eng].append(fn)
            return None
        sem, amt = inc
        tok = self.bump(sem, amt)
        self.ops[eng].append(
            lambda E, f=fn, s=sem, a=amt: f(E).then_inc(self.sems[s], a))
        return tok

    def dma(self, eng, out, in_, sem):
        return self.op(eng, lambda E, o=out, i=in_: E.dma_start(out=o, in_=i),
                       inc=(sem, 16))

    _DS = {"sync": ("ldw", "syn"), "gpsimd": ("gpd", "gpn")}

    def dsig(self, eng):
        dma_sem, tok_sem = self._DS[eng]
        self.wait(eng, dma_sem, self.cnt.get(dma_sem, 0))
        tok = self.bump(tok_sem, 1)
        self.ops[eng].append(
            lambda E, s=tok_sem: E.sem_inc(self.sems[s], 1))
        return tok

    def barrier(self):
        self.dsig("sync")
        self.dsig("gpsimd")
        s = dict(self.cnt)
        def w(eng, sem):
            self.wait(eng, sem, s.get(sem, 0))
        for eng in ENGINES:
            for sem in ("syn", "gpn", "ten", "vec", "sca"):
                w(eng, sem)


def build_program(cfg, debug=False):
    D, H, T, TT = cfg["D"], cfg["H"], cfg["T"], cfg["TT"]
    KC, HC = D // 128, H // 128
    ECC = KC
    NT = T // TT
    EPS = 1e-5

    nc = bass.Bass("TRN2", target_bir_lowering=False)

    x_h = nc.dram_tensor("x", [128, KC * T], F32, kind="ExternalInput").ap()
    wi_h = nc.dram_tensor("wi", [128, ECC * KC * 128], F32, kind="ExternalInput").ap()
    wf_h = nc.dram_tensor("wf", [128, ECC * KC * 128], F32, kind="ExternalInput").ap()
    wg_h = nc.dram_tensor("wg", [128, ECC * KC * 128], F32, kind="ExternalInput").ap()
    wo_h = nc.dram_tensor("wo", [128, KC * ECC * 128], F32, kind="ExternalInput").ap()
    wqg_h = nc.dram_tensor("wqg", [128, HC * KC * 128], I32, kind="ExternalInput").ap()
    wqu_h = nc.dram_tensor("wqu", [128, HC * KC * 128], I32, kind="ExternalInput").ap()
    wqd_h = nc.dram_tensor("wqd", [128, KC * HC * 128], I32, kind="ExternalInput").ap()
    gam_h = nc.dram_tensor("gamma", [128, ECC], F32, kind="ExternalInput").ap()
    sg_h = nc.dram_tensor("sg", [128, HC], F32, kind="ExternalInput").ap()
    su_h = nc.dram_tensor("su", [128, HC], F32, kind="ExternalInput").ap()
    sd_h = nc.dram_tensor("sd", [128, KC], F32, kind="ExternalInput").ap()
    rmask_h = nc.dram_tensor("rmask", [128, NCORES], F32, kind="ExternalInput").ap()
    rmm_h = nc.dram_tensor("rmm", [128, (NCORES - 1) * KC], F32,
                           kind="ExternalInput").ap()
    rmc_h = nc.dram_tensor("rmc", [128, (NCORES - 1) * KC], F32,
                           kind="ExternalInput").ap()
    out_h = nc.dram_tensor("out", [128, KC * T], F32, kind="ExternalOutput").ap()

    spk = "ExternalOutput" if debug else "Internal"
    d_o = nc.dram_tensor("d_o", [NT, 128, KC * TT], F32, kind=spk).ap()
    d_p = nc.dram_tensor("d_p", [NT, 128, KC * TT], F32, kind=spk).ap()
    d_yh = nc.dram_tensor("d_yh", [NT, 128, KC * TT], BF16, kind=spk).ap()
    d_yl = nc.dram_tensor("d_yl", [NT, 128, KC * TT], BF16, kind=spk).ap()
    d_x2 = nc.dram_tensor("d_x2", [NT, 128, KC * TT], F32, kind=spk).ap()
    d_cr = (nc.dram_tensor("d_cr", [128, KC], F32, kind="ExternalOutput").ap()
            if debug else None)
    d_gate = (nc.dram_tensor("d_gate", [NT, 128, KC * TT], F32,
                             kind="ExternalOutput").ap() if debug else None)
    d_iin = (nc.dram_tensor("d_iin", [NT, 128, KC * TT], F32,
                            kind="ExternalOutput").ap() if debug else None)
    d_init = (nc.dram_tensor("d_init", [NT, 128, 4 * KC], F32,
                             kind="ExternalOutput").ap() if debug else None)
    d_as = (nc.dram_tensor("d_as", [128, NCORES * 2 * KC], F32,
                           kind="ExternalOutput").ap() if debug else None)
    d_rm = (nc.dram_tensor("d_rm", [128, NCORES], F32,
                           kind="ExternalOutput").ap() if debug else None)
    d_vs = (nc.dram_tensor("d_vs", [128, NCORES * 2 * KC], F32,
                           kind="ExternalOutput").ap() if debug else None)
    d_h1 = (nc.dram_tensor("d_h1", [NT, 128, KC * TT], F32,
                           kind="ExternalOutput").ap() if debug else None)
    d_h2 = (nc.dram_tensor("d_h2", [NT, 128, KC * TT], F32,
                           kind="ExternalOutput").ap() if debug else None)
    cc_in = nc.dram_tensor("cc_in", [128, 2 * ECC], F32, kind="Internal").ap()
    cc_out = nc.dram_tensor("cc_out", [NCORES, 128, 2 * ECC], F32,
                            kind="Internal").ap()

    p = Prog()
    es = ExitStack()
    with es:
        SZ = KC * TT
        PAIR = KC * TT // 2
        W5 = 2 * KC * 128
        MISC = 11 * TT
        R7 = max(4 * (KC * 128 // 2), max(0, HC * TT - 2 * SZ))
        offs, o_ = {}, 0
        for name, sz in (("R0", SZ), ("R1", SZ), ("R2", SZ), ("R3", PAIR),
                         ("R4", PAIR), ("R5", W5), ("R6", MISC), ("R7", R7)):
            offs[name] = (o_, sz)
            o_ += sz
        mega = es.enter_context(nc.sbuf_tensor("mega", [128, o_], F32))

        def regf(name, off, n):
            a, sz = offs[name]
            assert off + n <= sz, (name, off, n, sz)
            return mega[:, a + off:a + off + n]

        def reg(name):
            return regf(name, 0, offs[name][1])

        def regb(name, off, n):
            a, sz = offs[name]
            assert off + n <= sz
            return mega[:, a + off:a + off + n].bitcast(BF16)

        def rch(name, c):
            return regf(name, c * TT, TT)

        def pch(name, c):
            return regb(name, c * TT // 2, TT // 2)

        hchunks = []
        for rn in ("R0", "R1", "R7"):
            a, sz = offs[rn]
            for off in range(0, sz - TT // 2 + 1, TT // 2):
                hchunks.append((rn, off))
        assert len(hchunks) >= 2 * HC, (len(hchunks), HC)

        def hhv(i):
            rn, off = hchunks[i]
            return regb(rn, off, TT // 2)

        def hlv(i):
            rn, off = hchunks[HC + i]
            return regb(rn, off, TT // 2)

        def msc(off, n):
            return regf("R6", off, n)

        scr_t = [msc(0, TT), msc(TT, TT)]
        scr_u = [msc(2 * TT, TT), msc(3 * TT, TT)]
        bm_t = msc(4 * TT, TT)
        br_t = msc(5 * TT, TT)
        bc_t = msc(6 * TT, TT)
        scr_v = [msc(7 * TT, TT), msc(8 * TT, TT)]
        scr_w = [msc(9 * TT, TT), msc(10 * TT, TT)]

        WBF = KC * 128 // 2

        def whiv(b):
            return regb("R7", (2 * b) * WBF, WBF)

        def wlov(b):
            return regb("R7", (2 * b + 1) * WBF, WBF)

        def wstv(b):
            return regf("R5", b * KC * 128, KC * 128)

        def ffnv(b):
            return regb("R5", b * WBF, WBF)

        DBF = HC * 128 // 2
        assert DBF <= PAIR

        def wdnv(b):
            return regb("R3" if b == 0 else "R4", 0, DBF)

        zeros = es.enter_context(nc.sbuf_tensor("zeros", [128, TT], F32))
        ones_c = es.enter_context(nc.sbuf_tensor("ones_c", [128, 1], F32))
        ones_r = es.enter_context(nc.sbuf_tensor("ones_r", [1, 128], F32))
        init_o = es.enter_context(nc.sbuf_tensor("init_o", [128, ECC], F32))
        init_p = es.enter_context(nc.sbuf_tensor("init_p", [128, ECC], F32))
        summ = es.enter_context(nc.sbuf_tensor("summ", [128, 2 * ECC], F32))
        asum = es.enter_context(nc.sbuf_tensor("asum", [128, NCORES * 2 * ECC], F32))
        carry = es.enter_context(nc.sbuf_tensor("carry", [128, ECC], F32))
        ctmp = es.enter_context(nc.sbuf_tensor("ctmp", [128, 3 * ECC], F32))
        gam_s = es.enter_context(nc.sbuf_tensor("gam_s", [128, ECC], F32))
        sg_s = es.enter_context(nc.sbuf_tensor("sg_s", [128, HC], F32))
        su_s = es.enter_context(nc.sbuf_tensor("su_s", [128, HC], F32))
        sd_s = es.enter_context(nc.sbuf_tensor("sd_s", [128, KC], F32))
        rm_s = es.enter_context(nc.sbuf_tensor("rm_s", [128, NCORES], F32))
        rmm_s = es.enter_context(nc.sbuf_tensor("rmm_s", [128, (NCORES - 1) * ECC], F32))
        rmc_s = es.enter_context(nc.sbuf_tensor("rmc_s", [128, (NCORES - 1) * ECC], F32))
        eps_s = es.enter_context(nc.sbuf_tensor("eps_s", [1, 1], F32))
        stv = es.enter_context(nc.sbuf_tensor("stv", [1, 6 * TT], F32))
        vsnap = (es.enter_context(nc.sbuf_tensor("vsnap",
                 [128, NCORES * 2 * ECC], F32)) if debug else None)

        def sv(i):
            return stv[0:1, i * TT:(i + 1) * TT]

        ps_full = [es.enter_context(nc.psum_tensor(f"ps{i}", [128, 512], F32))
                   for i in range(8)]
        ps = [t[:, 0:TT] for t in ps_full]

        sems = {}
        for s in ("ldw", "gpd", "ten", "vec", "sca", "cc", "syn", "gpn"):
            sems[s] = es.enter_context(nc.semaphore(s))
        p.sems = sems

        p.dma("sync", gam_s[:, :], gam_h, "ldw")
        p.dma("sync", sg_s[:, :], sg_h, "ldw")
        p.dma("sync", su_s[:, :], su_h, "ldw")
        p.dma("sync", sd_s[:, :], sd_h, "ldw")
        p.dma("sync", rm_s[:, :], rmask_h, "ldw")
        p.dma("sync", rmm_s[:, :], rmm_h, "ldw")
        p.dma("sync", rmc_s[:, :], rmc_h, "ldw")
        s_statics = p.dsig("sync")
        p.op("vector", lambda E: E.memset(zeros[:, :], 0.0))
        p.op("vector", lambda E: E.memset(ones_c[:, :], 1.0))
        p.op("vector", lambda E: E.memset(ones_r[:, :], 1.0))
        p.op("vector", lambda E: E.memset(init_o[:, :], 0.0))
        p.op("vector", lambda E: E.memset(eps_s[:, :], EPS))
        t_init = p.op("vector", lambda E: E.memset(init_p[:, :], 1.0),
                      inc=("vec", 1))

        def load_x(st, region, after=()):
            toks = {}
            for s_, v_ in after:
                p.wait("sync", s_, v_)
            for q in range(0, KC, 4):
                for kc in range(q, min(q + 4, KC)):
                    p.dma("sync", rch(region, kc),
                          x_h[:, kc * T + st * TT: kc * T + st * TT + TT],
                          "ldw")
                toks[q // 4] = p.dsig("sync")
            return toks

        def ln_stats(region, t_ready, rdy_sem):
            toks = t_ready if isinstance(t_ready, dict) else None

            def rdy(eng, kc):
                if toks is not None:
                    p.wait(eng, rdy_sem, toks[kc // 4])
                elif t_ready is not None:
                    p.wait(eng, rdy_sem, t_ready)
            p.wait("tensor", "vec", t_init)
            t_sum = None
            for kc in range(KC):
                rdy("tensor", kc)
                t_sum = p.op("tensor", lambda E, k=kc: E.matmul(
                    ps[5][0:1, :], ones_c[:, :], rch(region, k),
                    start=(k == 0), stop=(k == KC - 1)),
                    inc=("ten", 1) if kc == KC - 1 else None)
            sq_mm = {}
            t_sq = None
            for kc in range(KC):
                if kc >= 2:
                    p.wait("scalar", "ten", sq_mm[kc - 2])
                rdy("scalar", kc)
                tk = p.op("scalar", lambda E, k=kc: E.activation(
                    out=scr_u[k % 2][:, :], in_=rch(region, k), func=AF.Square),
                    inc=("sca", 1))
                p.wait("tensor", "sca", tk)
                t_sq = sq_mm[kc] = p.op("tensor", lambda E, k=kc: E.matmul(
                    ps[4][0:1, :], ones_c[:, :], scr_u[k % 2][:, :],
                    start=(k == 0), stop=(k == KC - 1)), inc=("ten", 1))
            p.wait("vector", "ten", max(t_sum, t_sq))
            wl0 = p.op("vector", lambda E: E.tensor_scalar(
                out=sv(0), in0=ps[5][0:1, :], scalar1=1.0 / D, scalar2=None,
                op0=OP.mult), inc=("vec", 1))
            p.wait("vector", "vec", wl0)
            wls = p.op("vector", lambda E: E.tensor_scalar(
                out=sv(1), in0=ps[4][0:1, :], scalar1=1.0 / D, scalar2=None,
                op0=OP.mult), inc=("vec", 1))
            p.wait("vector", "vec", wls)
            p.op("vector", lambda E: E.tensor_mul(sv(2), sv(0), sv(0)))
            t_var = p.op("vector", lambda E: E.tensor_sub(sv(3), sv(1), sv(2)),
                         inc=("vec", 1))
            t_r = _rsqrt_vec(sv(3), t_var, 1, 2)
            p.wait("tensor", "vec", t_r)
            p.op("tensor", lambda E: E.matmul(ps[6][:, :], ones_r[:, :], sv(0),
                                              start=True, stop=True))
            t_bc = p.op("tensor", lambda E: E.matmul(
                ps[7][:, :], ones_r[:, :], sv(5), start=True, stop=True),
                inc=("ten", 1))
            p.wait("vector", "ten", t_bc)
            p.op("vector", lambda E: E.tensor_copy(out=bm_t, in_=ps[6][:, :]))
            return p.op("vector", lambda E: E.tensor_copy(out=br_t, in_=ps[7][:, :]),
                        inc=("vec", 1))

        def _rsqrt_vec(v, t_v, ia, ib):
            a, b = sv(ia), sv(ib)
            w0 = p.op("vector", lambda E: E.tensor_scalar(
                out=sv(4), in0=v, scalar1=EPS, scalar2=None, op0=OP.add),
                inc=("vec", 1))
            p.wait("scalar", "vec", p.cnt["vec"])
            t_sqr = p.op("scalar", lambda E: E.activation(
                out=sv(5), in_=v, func=AF.Sqrt, bias=eps_s[0:1, 0:1]),
                inc=("sca", 1))
            p.wait("vector", "sca", t_sqr)
            p.wait("vector", "vec", w0)
            p.op("vector", lambda E: E.reciprocal(a, sv(5)))
            p.op("vector", lambda E: E.tensor_mul(b, a, a))
            p.op("vector", lambda E: E.tensor_mul(sv(5), b, sv(4)))
            w1 = p.op("vector", lambda E: E.tensor_scalar(
                out=b, in0=sv(5), scalar1=-0.5, scalar2=1.5,
                op0=OP.mult, op1=OP.add), inc=("vec", 1))
            p.wait("vector", "vec", w1)
            return p.op("vector", lambda E: E.tensor_mul(sv(5), a, b),
                        inc=("vec", 1))

        import os as _os
        _noln = bool(_os.environ.get("KDBG_NOLN"))

        def ln_apply_split(region, t_stats):
            p.wait("vector", "vec", t_stats)
            tk = None
            for kc in range(KC):
                s = scr_t[kc % 2]
                if not _noln:
                    p.op("vector", lambda E, k=kc, s=s: E.tensor_sub(
                        s, rch(region, k), bm_t))
                    p.op("vector", lambda E, k=kc, s=s: E.tensor_mul(
                        rch(region, k), s, br_t))
                p.op("vector", lambda E, k=kc: E.tensor_copy(
                    out=pch("R3", k), in_=rch(region, k)))
                tk = p.op("vector", lambda E, k=kc: E.tensor_tensor(
                    out=pch("R4", k), in0=rch(region, k), in1=pch("R3", k),
                    op=OP.subtract), inc=("vec", 1))
            return tk

        def projection(w_h, n_out, nk, rhs_hi, rhs_lo, evac, rhs_waits,
                       first_waits=None):
            snap0 = dict(p.cnt)
            state = {}

            def stage(oc):
                b = oc % 2
                wst, whi, wlo = wstv(b), whiv(b), wlov(b)
                prev = state.get(("split", b))
                if prev is not None:
                    p.wait("sync", "vec", prev)
                elif first_waits is not None:
                    for s_, v_ in first_waits["stage"]:
                        p.wait("sync", s_, v_)
                else:
                    p.wait("sync", "vec", snap0.get("vec", 0))
                    p.wait("sync", "sca", snap0.get("sca", 0))
                p.dma("sync", wst,
                      w_h[:, oc * nk * 128:(oc + 1) * nk * 128], "ldw")
                s_w = p.dsig("sync")
                tb = state.get(("used", b), snap0.get("ten", 0))
                p.wait("vector", "syn", s_w)
                p.wait("vector", "ten", tb)
                p.op("vector", lambda E, w=wst, h=whi: E.tensor_copy(
                    out=h, in_=w))
                t_lo = p.op("vector", lambda E, w=wst, h=whi, l=wlo:
                            E.tensor_tensor(out=l, in0=w, in1=h,
                                            op=OP.subtract), inc=("vec", 1))
                state[("split", b)] = t_lo
                state[("ready", oc)] = t_lo

            stage(0)
            if n_out > 1:
                stage(1)
            for oc in range(n_out):
                b = oc % 2
                whi, wlo = whiv(b), wlov(b)
                p.wait("tensor", "vec", state[("ready", oc)])
                if oc == 0:
                    for s_, v_ in rhs_waits:
                        p.wait("tensor", s_, v_)
                bf = state.get(("bank", b))
                if bf is not None:
                    p.wait("tensor", bf[0], bf[1])
                elif first_waits is not None:
                    for s_, v_ in first_waits["bank"]:
                        p.wait("tensor", s_, v_)
                else:
                    p.wait("tensor", "sca", snap0.get("sca", 0))
                    p.wait("tensor", "vec", snap0.get("vec", 0))
                t_grp = None
                pb = ps[b]
                for kc in range(nk):
                    wh = whi[:, kc * 128:(kc + 1) * 128]
                    wl = wlo[:, kc * 128:(kc + 1) * 128]
                    p.op("tensor", lambda E, a=pb, w=wh, r=rhs_hi(kc), st=(kc == 0):
                         E.matmul(a[:, :], w, r, start=st, stop=False))
                    p.op("tensor", lambda E, a=pb, w=wh, r=rhs_lo(kc):
                         E.matmul(a[:, :], w, r, start=False, stop=False))
                    t_grp = p.op(
                        "tensor",
                        lambda E, a=pb, w=wl, r=rhs_hi(kc), sp=(kc == nk - 1):
                        E.matmul(a[:, :], w, r, start=False, stop=sp),
                        inc=("ten", 1) if kc == nk - 1 else None)
                state[("used", b)] = t_grp
                state[("bank", b)] = evac(oc, ps[b][:, :], t_grp)
                if oc + 2 < n_out:
                    stage(oc + 2)
            return state

        spill_tok = {}
        for st in range(NT):
            if st == 0:
                p.barrier()
            t_x = load_x(st, "R0",
                         after=([("gpn", spill_tok[("o", st - 1)])]
                                if st > 0 else ()))
            if _os.environ.get("KDBG_NOSTATS"):
                p.wait("vector", "syn", max(t_x.values()))
                t_st = p.op("vector", lambda E: E.engine_nop(), inc=("vec", 1))
            else:
                t_st = ln_stats("R0", t_x, "syn")
            if st > 0:
                p.wait("vector", "gpn", spill_tok[("rest", st - 1)])
                p.wait("scalar", "gpn", spill_tok[("rest", st - 1)])
            t_pair = ln_apply_split("R0", t_st)

            evF = {}

            def evac_f(oc, psap, t_grp):
                p.wait("scalar", "ten", t_grp)
                import os as _os
                if _os.environ.get("KDBG_NOSIG"):
                    tk = p.op("scalar", lambda E, o=oc, a=psap: E.copy(
                        out=rch("R1", o), in_=a), inc=("sca", 1))
                else:
                    tk = p.op("scalar", lambda E, o=oc, a=psap: E.activation(
                        out=rch("R1", o), in_=a, func=AF.Sigmoid), inc=("sca", 1))
                evF[oc] = tk
                return ("sca", tk)

            projection(wf_h, ECC, KC, lambda kc: pch("R3", kc),
                       lambda kc: pch("R4", kc), evac_f,
                       rhs_waits=[("vec", t_pair)])

            evI = {}
            t_scan_tok = {}

            def evac_i(oc, psap, t_grp):
                p.wait("scalar", "ten", t_grp)
                if oc >= 2:
                    p.wait("scalar", "vec", evI[oc - 2])
                t_si = p.op("scalar", lambda E, o=oc, a=psap: E.activation(
                    out=scr_u[o % 2][:, :], in_=a, func=AF.Silu), inc=("sca", 1))
                p.wait("vector", "sca", t_si)
                tk = p.op("vector", lambda E, o=oc: E.scalar_tensor_tensor(
                    out=rch("R2", o), in0=rch("R1", o), scalar=1.0,
                    in1=scr_u[o % 2][:, :], op0=OP.subtract, op1=OP.mult),
                    inc=("vec", 1))
                evI[oc] = tk
                p.wait("vector", "sca", evF[oc])
                p.wait("vector", "vec", tk)
                tso = p.op("vector", lambda E, e=oc: E.tensor_tensor_scan(
                    out=rch("R0", e), data0=rch("R1", e), data1=rch("R2", e),
                    initial=init_o[:, e:e + 1], op0=OP.mult, op1=OP.subtract),
                    inc=("vec", 1))
                p.wait("vector", "vec", tso)
                p.op("vector", lambda E, e=oc: E.tensor_copy(
                    out=init_o[:, e:e + 1], in_=rch("R0", e)[:, TT - 1:TT]))
                tsp = p.op("vector", lambda E, e=oc: E.tensor_tensor_scan(
                    out=rch("R2", e), data0=rch("R1", e), data1=zeros[:, :],
                    initial=init_p[:, e:e + 1], op0=OP.mult, op1=OP.add),
                    inc=("vec", 1))
                p.wait("vector", "vec", tsp)
                t_scan_tok[oc] = p.op("vector", lambda E, e=oc: E.tensor_copy(
                    out=init_p[:, e:e + 1], in_=rch("R2", e)[:, TT - 1:TT]),
                    inc=("vec", 1))
                return ("sca", t_si)

            projection(wi_h, ECC, KC, lambda kc: pch("R3", kc),
                       lambda kc: pch("R4", kc), evac_i,
                       rhs_waits=[("vec", t_pair)])
            t_scan = t_scan_tok[ECC - 1]

            if st < NT - 1:
                for q in range(0, ECC, 4):
                    hi = min(q + 4, ECC) - 1
                    p.wait("gpsimd", "vec", t_scan_tok[hi])
                    p.dma("gpsimd", d_o[st][:, q * TT:(hi + 1) * TT],
                          regf("R0", q * TT, (hi + 1 - q) * TT), "gpd")
                spill_tok[("o", st)] = p.dsig("gpsimd")
                p.dma("gpsimd", d_p[st], reg("R2"), "gpd")
                p.dma("gpsimd", d_yh[st], reg("R3").bitcast(BF16), "gpd")
                p.dma("gpsimd", d_yl[st], reg("R4").bitcast(BF16), "gpd")
                spill_tok[("rest", st)] = p.dsig("gpsimd")

        p.wait("vector", "vec", t_scan)
        p.op("vector", lambda E: E.tensor_copy(out=summ[:, 0:ECC],
                                               in_=init_p[:, :]))
        t_summ = p.op("vector", lambda E: E.tensor_copy(
            out=summ[:, ECC:2 * ECC], in_=init_o[:, :]), inc=("vec", 1))
        p.barrier()
        snapP2 = dict(p.cnt)
        p.wait("gpsimd", "vec", t_summ)
        p.dma("gpsimd", cc_in, summ[:, :], "gpd")
        p.wait("gpsimd", "gpd", p.cnt["gpd"])
        t_cc = p.op("gpsimd", lambda E: E.collective_compute(
            "AllGather", OP.bypass, replica_groups=[list(range(NCORES))],
            ins=[cc_in], outs=[cc_out]), inc=("cc", 1))
        p.wait("gpsimd", "cc", t_cc)
        for r in range(NCORES):
            p.dma("gpsimd", asum[:, r * 2 * ECC:(r + 1) * 2 * ECC],
                  cc_out[r], "gpd")
        g_as = p.dsig("gpsimd")
        if debug:
            p.dma("gpsimd", d_as, asum[:, :], "gpd")
            p.dma("gpsimd", d_rm, rm_s[:, :], "gpd")

        def gop(fn):
            tok = p.op("gpsimd", fn, inc=("gpn", 1))
            p.wait("gpsimd", "gpn", tok)
            return tok
        gop(lambda E: E.memset(carry[:, :], 0.0))
        for r in range(NCORES - 1):
            A_r = asum[:, r * 2 * ECC: r * 2 * ECC + ECC]
            B_r = asum[:, r * 2 * ECC + ECC: (r + 1) * 2 * ECC]
            mm_r = rmm_s[:, r * ECC:(r + 1) * ECC]
            mc_r = rmc_s[:, r * ECC:(r + 1) * ECC]
            tA, tB, tC = (ctmp[:, 0:ECC], ctmp[:, ECC:2 * ECC],
                          ctmp[:, 2 * ECC:3 * ECC])
            gop(lambda E, A=A_r, m=mm_r, t=tC: E.tensor_mul(t, A, m))
            gop(lambda E, t=tA, c=tC, m=mc_r: E.tensor_add(t, c, m))
            gop(lambda E, B=B_r, m=mm_r, t=tB: E.tensor_mul(t, B, m))
            gop(lambda E, t=tC, a=tA: E.tensor_mul(t, carry[:, :], a))
            gop(lambda E, t=tC, b=tB: E.tensor_add(carry[:, :], t, b))
        t_carry = gop(lambda E: E.tensor_copy(out=ctmp[:, 0:ECC],
                                              in_=carry[:, :]))
        if debug:
            p.dma("gpsimd", d_cr, carry[:, :], "gpd")

        def prh(rn, ec):
            return regf(rn, ec * TT, TT // 2).bitcast(BF16)

        def prl(rn, ec):
            return regf(rn, ec * TT + TT // 2, TT // 2).bitcast(BF16)

        assert NT == 2
        P2MAPS = {1: dict(o="R0", pp="R2", g="R1"),
                  0: dict(o="R2", pp="R1", g="R0")}

        def p2_body(st, first, rd_toks, extra_gwait, prefetch,
                    g_first_waits=None):
            Ro, Rp, Rg = (P2MAPS[st][k] for k in ("o", "pp", "g"))
            t_rdp, t_rd = rd_toks if rd_toks else (None, None)
            if first:
                for eng in ENGINES:
                    for sem in ("syn", "gpn", "ten", "vec", "sca"):
                        p.wait(eng, sem, snapP2.get(sem, 0))
            else:
                p.barrier()

            g2mm = {}
            g2sq = {}
            t_sil = {}
            t_oa_tok = {}
            DLY = min(3, ECC)

            def emit_g2(o):
                p.wait("tensor", "sca", g2sq[o])
                g2mm[o] = p.op("tensor", lambda E, o=o: E.matmul(
                    ps[4][0:1, :], ones_c[:, :], scr_u[o % 2][:, :],
                    start=(o == 0), stop=(o == ECC - 1)), inc=("ten", 1))

            def chain(ec):
                u = scr_w[ec % 2]
                if ec == 0:
                    p.wait("vector", "gpn", t_carry)
                    if t_rd is not None:
                        p.wait("vector", "gpn", t_rd)
                if ec >= 2:
                    p.wait("vector", "sca", t_sil[ec - 2])
                tf = p.op("vector", lambda E, e=ec, u=u: E.scalar_tensor_tensor(
                    out=u, in0=rch(Rp, e), scalar=carry[:, e:e + 1],
                    in1=rch(Ro, e), op0=OP.mult, op1=OP.add), inc=("vec", 1))
                p.wait("scalar", "vec", tf)
                t_sil[ec] = p.op("scalar", lambda E, e=ec, u=u: E.activation(
                    out=rch(Rp, e), in_=u, func=AF.Silu), inc=("sca", 1))
                p.wait("vector", "sca", t_sil[ec])
                p.op("vector", lambda E, e=ec: E.tensor_mul(
                    rch(Ro, e), rch(Rg, e), rch(Rp, e)))
                whi_ = p.op("vector", lambda E, e=ec: E.tensor_scalar(
                    out=prh(Rp, e), in0=rch(Ro, e),
                    scalar1=gam_s[:, e:e + 1], scalar2=None, op0=OP.mult),
                    inc=("vec", 1))
                p.wait("vector", "vec", whi_)
                t_oa_tok[ec] = p.op("vector", lambda E, e=ec: E.scalar_tensor_tensor(
                    out=prl(Rp, e), in0=rch(Ro, e),
                    scalar=gam_s[:, e:e + 1], in1=prh(Rp, e),
                    op0=OP.mult, op1=OP.subtract), inc=("vec", 1))

            def evac_g(oc, psap, t_grp):
                p.wait("vector", "ten", t_grp)
                if oc == 0 and extra_gwait is not None:
                    p.wait("vector", "gpn", extra_gwait)
                t_cp = p.op("vector", lambda E, o=oc, a=psap: E.tensor_copy(
                    out=rch(Rg, o), in_=a), inc=("vec", 1))
                if oc >= 2:
                    p.wait("scalar", "ten", g2mm[oc - 2])
                p.wait("scalar", "vec", t_cp)
                g2sq[oc] = p.op("scalar", lambda E, o=oc: E.activation(
                    out=scr_u[o % 2][:, :], in_=rch(Rg, o), func=AF.Square),
                    inc=("sca", 1))
                if oc >= 1:
                    emit_g2(oc - 1)
                if oc >= DLY:
                    chain(oc - DLY)
                return ("vec", t_cp)

            gw = [("gpn", t_rdp)] if t_rdp is not None else []
            projection(wg_h, ECC, KC, lambda kc: pch("R3", kc),
                       lambda kc: pch("R4", kc), evac_g, rhs_waits=gw,
                       first_waits=g_first_waits)
            t_glast = p.cnt["ten"]
            if prefetch is not None:
                prefetch("after_g", ten=t_glast)
            emit_g2(ECC - 1)
            p.wait("vector", "ten", g2mm[ECC - 1])
            t_ms = p.op("vector", lambda E: E.tensor_scalar(
                out=sv(1), in0=ps[4][0:1, :], scalar1=1.0 / D, scalar2=None,
                op0=OP.mult), inc=("vec", 1))
            t_r2 = _rsqrt_vec(sv(1), t_ms, 2, 3)
            p.wait("tensor", "vec", t_r2)
            t_bc2 = p.op("tensor", lambda E: E.matmul(
                ps[6][:, :], ones_r[:, :], sv(5), start=True, stop=True),
                inc=("ten", 1))
            p.wait("vector", "ten", t_bc2)
            t_brd = p.op("vector", lambda E: E.tensor_copy(
                out=br_t, in_=ps[6][:, :]), inc=("vec", 1))

            for ec in range(max(0, ECC - DLY), ECC):
                chain(ec)
            t_oa = t_oa_tok[ECC - 1]

            p.wait("gpsimd", "vec", t_oa)
            for kc in range(KC):
                p.dma("gpsimd", rch(Rg, kc),
                      x_h[:, kc * T + st * TT: kc * T + st * TT + TT], "gpd")
            t_xr = p.dsig("gpsimd")

            ew_tok = {}

            def evac_wo(oc, psap, t_grp):
                p.wait("vector", "ten", t_grp)
                if oc == 0:
                    p.wait("vector", "gpn", t_xr)
                    p.wait("vector", "vec", t_brd)
                s = scr_t[oc % 2]
                p.op("vector", lambda E, o=oc, a=psap, s=s: E.tensor_mul(
                    s, a, br_t))
                tk = p.op("vector", lambda E, o=oc, s=s: E.tensor_add(
                    rch(Ro, o), s, rch(Rg, o)), inc=("vec", 1))
                ew_tok[oc] = tk
                return ("vec", tk)

            wo_state = projection(wo_h, KC, ECC, lambda ec: prh(Rp, ec),
                                  lambda ec: prl(Rp, ec), evac_wo,
                                  rhs_waits=[("vec", t_oa)])
            t_wolast_ten = p.cnt["ten"]
            t_wolast_vec = p.cnt["vec"]
            if prefetch is not None:
                prefetch("after_wo", ten=t_wolast_ten, vec=t_wolast_vec)
            return t_wolast_vec, ew_tok, wo_state

        pf_toks = {}

        def prefetch_st0(stage_name, ten=None, vec=None):
            if stage_name == "after_g":
                p.wait("gpsimd", "ten", ten)
                p.dma("gpsimd", reg("R3").bitcast(BF16), d_yh[0], "gpd")
                p.dma("gpsimd", reg("R4").bitcast(BF16), d_yl[0], "gpd")
                pf_toks["rdp"] = p.dsig("gpsimd")
            else:
                p.wait("gpsimd", "ten", ten)
                p.dma("gpsimd", reg("R2"), d_o[0], "gpd")
                p.wait("gpsimd", "vec", vec)
                p.dma("gpsimd", reg("R1"), d_p[0], "gpd")
                pf_toks["rd"] = p.dsig("gpsimd")

        t_x2v, ew1, wo1 = p2_body(NT - 1, True, None, None, prefetch_st0)
        p.wait("gpsimd", "vec", t_x2v)
        p.dma("gpsimd", d_x2[1], reg("R0"), "gpd")
        t_x2sp = p.dsig("gpsimd")

        tight = {"stage": [("vec", wo1[("ready", KC - 1)])],
                 "bank": [("vec", ew1[KC - 1])]}
        _, ew0, _ = p2_body(0, False, (pf_toks["rdp"], pf_toks["rd"]),
                            t_x2sp, None, g_first_waits=tight)

        prev_ev = [None]
        out_tok_prev = [None]
        for st in range(NT):
            if st > 0:
                ev0 = prev_ev[0]
                snapP3b = dict(p.cnt)
                for eng in ("gpsimd", "scalar", "tensor", "vector"):
                    for sem in ("ten", "vec", "sca"):
                        p.wait(eng, sem, snapP3b.get(sem, 0))
                p.wait("sync", "gpn", snapP3b.get("gpn", 0))
                rd_toks3 = {}
                for q in range(0, KC, 4):
                    hi = min(q + 4, KC) - 1
                    p.wait("sync", "vec", ev0[hi])
                    p.wait("sync", "syn", out_tok_prev[0][hi])
                    p.dma("sync", d_x2[st][:, q * TT:(hi + 1) * TT] if False
                          else reg("R2")[:, q * TT:(hi + 1) * TT],
                          d_x2[st][:, q * TT:(hi + 1) * TT], "ldw")
                    rd_toks3[q // 4] = p.dsig("sync")
                t_st3 = ln_stats("R2", rd_toks3, "syn")
            else:
                snapP3 = dict(p.cnt)
                p.wait("gpsimd", "sca", snapP3.get("sca", 0))
                p.wait("gpsimd", "ten", snapP3.get("ten", 0))
                p.wait("gpsimd", "vec", snapP3.get("vec", 0))
                ew_batches = {q: ew0[min(q * 4 + 3, KC - 1)]
                              for q in range((KC + 3) // 4)}
                t_st3 = ln_stats("R2", ew_batches, "vec")
            p.wait("vector", "vec", t_st3)
            t_p3 = None
            for kc in range(KC):
                s, u = scr_t[kc % 2], scr_w[kc % 2]
                p.op("vector", lambda E, k=kc, s=s: E.tensor_sub(
                    s, rch("R2", k), bm_t))
                p.op("vector", lambda E, s=s, u=u: E.tensor_mul(u, s, br_t))
                p.op("vector", lambda E, k=kc, u=u: E.tensor_copy(
                    out=pch("R3", k), in_=u))
                t_p3 = p.op("vector", lambda E, k=kc, u=u: E.tensor_tensor(
                    out=pch("R4", k), in0=u, in1=pch("R3", k), op=OP.subtract),
                    inc=("vec", 1))

            wu_used = {}
            sg_tok = {}
            hf_tok = {}
            cs_tok = {}
            t_h = {}
            gbank = {}
            ubank = {}

            def emit_cs(h):
                p.wait("tensor", "vec", t_h[h])
                cs_tok[h] = p.op("tensor", lambda E, h=h, b=scr_w[h % 2]:
                                 E.matmul(ps[4][0:1, :], ones_c[:, :], b,
                                          start=(h == 0), stop=(h == HC - 1)),
                                 inc=("ten", 1))

            for hc in range(HC):
                bg, bu = hc % 2, 2 + hc % 2
                tb = wu_used.get(bg)
                if tb is not None:
                    p.wait("gpsimd", "ten", tb)
                p.dma("gpsimd", ffnv(bg),
                      wqg_h[:, hc * KC * 128:(hc + 1) * KC * 128], "gpd")
                p.dma("gpsimd", ffnv(2 + bg),
                      wqu_h[:, hc * KC * 128:(hc + 1) * KC * 128], "gpd")
                t_w2 = p.dsig("gpsimd")
                p.wait("tensor", "gpn", t_w2)
                if hc == 0:
                    p.wait("tensor", "vec", t_p3)
                bf = gbank.get(bg)
                if bf is not None:
                    p.wait("tensor", "sca", bf)
                t_gg = None
                for kc in range(KC):
                    w = ffnv(bg)[:, kc * 128:(kc + 1) * 128]
                    p.op("tensor", lambda E, a=ps[bg], w=w, k=kc: E.matmul(
                        a[:, :], w, pch("R3", k), start=(k == 0),
                        stop=False))
                    t_gg = p.op("tensor", lambda E, a=ps[bg], w=w, k=kc: E.matmul(
                        a[:, :], w, pch("R4", k), start=False,
                        stop=(k == KC - 1)),
                        inc=("ten", 1) if kc == KC - 1 else None)
                p.wait("scalar", "ten", t_gg)
                if hc >= 2:
                    p.wait("scalar", "vec", hf_tok[hc - 2])
                t_sg2 = p.op("scalar", lambda E, h=hc, b=bg: E.activation(
                    out=scr_u[h % 2][:, :], in_=ps[b][:, :], func=AF.Silu,
                    scale=sg_s[:, h:h + 1]), inc=("sca", 1))
                sg_tok[hc] = t_sg2
                gbank[bg] = t_sg2
                bf = ubank.get(bu)
                if bf is not None:
                    p.wait("tensor", "vec", bf)
                t_ug = None
                for kc in range(KC):
                    w = ffnv(2 + bg)[:, kc * 128:(kc + 1) * 128]
                    p.op("tensor", lambda E, a=ps[bu], w=w, k=kc: E.matmul(
                        a[:, :], w, pch("R3", k), start=(k == 0),
                        stop=False))
                    t_ug = p.op("tensor", lambda E, a=ps[bu], w=w, k=kc: E.matmul(
                        a[:, :], w, pch("R4", k), start=False,
                        stop=(k == KC - 1)),
                        inc=("ten", 1) if kc == KC - 1 else None)
                wu_used[bg] = t_ug
                p.wait("vector", "ten", t_ug)
                p.wait("vector", "sca", t_sg2)
                if hc >= 2:
                    p.wait("vector", "ten", cs_tok[hc - 2])
                hbuf = scr_w[hc % 2]
                t_hf = p.op("vector",
                            lambda E, h=hc, b=hbuf, u=bu: E.scalar_tensor_tensor(
                                out=b, in0=ps[u][:, :], scalar=su_s[:, h:h + 1],
                                in1=scr_u[h % 2][:, :], op0=OP.mult, op1=OP.mult),
                            inc=("vec", 1))
                hf_tok[hc] = t_hf
                ubank[bu] = t_hf
                p.wait("vector", "vec", t_hf)
                p.op("vector", lambda E, h=hc, b=hbuf: E.tensor_copy(
                    out=hhv(h), in_=b))
                t_h[hc] = p.op("vector", lambda E, h=hc, b=hbuf: E.tensor_tensor(
                    out=hlv(h), in0=b, in1=hhv(h), op=OP.subtract),
                    inc=("vec", 1))
                if hc >= 1:
                    emit_cs(hc - 1)

            emit_cs(HC - 1)
            t_cs = cs_tok[HC - 1]
            p.wait("vector", "ten", t_cs)
            t_cs2 = p.op("vector", lambda E: E.tensor_scalar(
                out=sv(0), in0=ps[4][0:1, :], scalar1=128.0, scalar2=None,
                op0=OP.mult), inc=("vec", 1))
            p.wait("tensor", "vec", t_cs2)
            t_bc3 = p.op("tensor", lambda E: E.matmul(
                ps[6][:, :], ones_r[:, :], sv(0), start=True, stop=True),
                inc=("ten", 1))
            p.wait("vector", "ten", t_bc3)
            t_bcd = p.op("vector", lambda E: E.tensor_copy(
                out=bc_t, in_=ps[6][:, :]), inc=("vec", 1))

            p.wait("tensor", "sca", p.cnt["sca"])
            p.wait("tensor", "vec", p.cnt["vec"])
            dn_used = {}
            dbank = {}
            out_tok = {}
            out_tok_prev[0] = out_tok
            ev_tok = prev_ev[0] = {}
            for dc in range(KC):
                b = dc % 2
                tb = dn_used.get(b)
                if tb is not None:
                    p.wait("gpsimd", "ten", tb)
                if dc < 2:
                    p.wait("gpsimd", "ten", t_cs)
                p.dma("gpsimd", wdnv(b),
                      wqd_h[:, dc * HC * 128:(dc + 1) * HC * 128], "gpd")
                t_wd = p.dsig("gpsimd")
                p.wait("tensor", "gpn", t_wd)
                bf = dbank.get(b)
                if bf is not None:
                    p.wait("tensor", "vec", bf)
                t_dg = None
                for hc in range(HC):
                    w = wdnv(b)[:, hc * 128:(hc + 1) * 128]
                    p.op("tensor", lambda E, a=ps[b], w=w, h=hc: E.matmul(
                        a[:, :], w, hhv(h), start=(h == 0), stop=False))
                    t_dg = p.op("tensor", lambda E, a=ps[b], w=w, h=hc: E.matmul(
                        a[:, :], w, hlv(h), start=False,
                        stop=(h == HC - 1)),
                        inc=("ten", 1) if hc == HC - 1 else None)
                dn_used[b] = t_dg
                p.wait("vector", "ten", t_dg)
                p.wait("vector", "vec", t_bcd)
                if dc >= 2:
                    p.wait("vector", "syn", out_tok[dc - 2])
                ob = scr_v[dc % 2]
                p.op("vector", lambda E, a=ps[b][:, :], o=ob: E.tensor_sub(
                    o, a, bc_t))
                t_ev = p.op("vector", lambda E, o=ob, d=dc: E.scalar_tensor_tensor(
                    out=o, in0=o, scalar=sd_s[:, d:d + 1], in1=rch("R2", d),
                    op0=OP.mult, op1=OP.add), inc=("vec", 1))
                dbank[b] = t_ev
                ev_tok[dc] = t_ev
                p.wait("sync", "vec", t_ev)
                p.dma("sync", out_h[:, dc * T + st * TT: dc * T + st * TT + TT],
                      ob, "ldw")
                out_tok[dc] = p.dsig("sync")

        p.wait("sync", "ldw", p.cnt["ldw"])
        p.wait("gpsimd", "gpd", p.cnt["gpd"])

        block = es.enter_context(nc.Block())

        def emit(name):
            def body(E):
                for fn in p.ops[name]:
                    fn(E)
            return body

        block.sync(emit("sync"))
        block.tensor(emit("tensor"))
        block.vector(emit("vector"))
        block.scalar(emit("scalar"))
        block.gpsimd(emit("gpsimd"))

    return nc



def _img_act(xs, T, KC):
    return np.ascontiguousarray(
        xs.reshape(T, KC, 128).transpose(2, 1, 0).reshape(128, KC * T))


def _img_w(WT, n_in_c, n_out_c):
    r = WT.reshape(n_in_c, 128, n_out_c, 128).transpose(1, 2, 0, 3)
    return np.ascontiguousarray(r.reshape(128, n_out_c * n_in_c * 128))


def _img_vec(v, nchunks):
    return np.ascontiguousarray(v.reshape(nchunks, 128).T)


def run(cfg, x, Wi, Wf, Wg, gam, Wo, wqg, sgv, wqu, suv, wqd, sdv,
        debug=False):
    D, H, T = cfg["D"], cfg["H"], cfg["T"]
    KC, HC = D // 128, H // 128
    wi_img = _img_w(np.ascontiguousarray(Wi.T), KC, KC)
    wf_img = _img_w(np.ascontiguousarray(Wf.T), KC, KC)
    wg_img = _img_w(np.ascontiguousarray(Wg.T), KC, KC)
    wo_img = _img_w(np.ascontiguousarray(Wo.T), KC, KC)
    wqg_img = _img_w(np.ascontiguousarray(wqg.T), KC, HC)
    wqu_img = _img_w(np.ascontiguousarray(wqu.T), KC, HC)
    wqd_img = _img_w(np.ascontiguousarray(wqd.T), HC, KC)
    gam_img = _img_vec(gam, KC)
    sg_img = _img_vec(sgv, HC)
    su_img = _img_vec(suv, HC)
    sd_img = _img_vec(sdv, KC)

    nc = build_program(cfg, debug=debug)
    in_maps = []
    for k in range(NCORES):
        xs = x[k * T:(k + 1) * T]
        rm = np.tile((np.arange(NCORES) < k).astype(np.float32), (128, 1))
        mvec = (np.arange(NCORES - 1) < k).astype(np.float32)
        rmm = np.tile(np.repeat(mvec, KC), (128, 1)).astype(np.float32)
        rmc = np.tile(np.repeat(1.0 - mvec, KC), (128, 1)).astype(np.float32)
        in_maps.append({
            "x": _img_act(xs, T, KC),
            "wi": wi_img, "wf": wf_img, "wg": wg_img, "wo": wo_img,
            "wqg": wqg_img, "wqu": wqu_img, "wqd": wqd_img,
            "gamma": gam_img, "sg": sg_img, "su": su_img, "sd": sd_img,
            "rmask": np.ascontiguousarray(rm),
            "rmm": np.ascontiguousarray(rmm),
            "rmc": np.ascontiguousarray(rmc),
        })
    res = run_bass_kernel_spmd(nc, in_maps, core_ids=list(range(NCORES)))
    outs = []
    for k in range(NCORES):
        o = np.asarray(res.results[k]["out"])
        outs.append(o.reshape(128, KC, T).transpose(2, 1, 0).reshape(T, D))
    full = np.concatenate(outs, axis=0)
    if debug:
        return full, res.results
    return full


def kernel(L0_attn_hidden_states, L0_attn_Wi, L0_attn_Wf, L0_attn_Wg,
           L0_attn_g_norm_weight, L0_attn_Wo, L0_ffn_wq_gate, L0_ffn_s_gate,
           L0_ffn_wq_up, L0_ffn_s_up, L0_ffn_wq_down, L0_ffn_s_down):
    cfg = CFG_REAL
    out = run(
        cfg,
        np.asarray(L0_attn_hidden_states, dtype=np.float32)[0],
        np.asarray(L0_attn_Wi, dtype=np.float32),
        np.asarray(L0_attn_Wf, dtype=np.float32),
        np.asarray(L0_attn_Wg, dtype=np.float32),
        np.asarray(L0_attn_g_norm_weight, dtype=np.float32),
        np.asarray(L0_attn_Wo, dtype=np.float32),
        np.asarray(L0_ffn_wq_gate, dtype=np.int32),
        np.asarray(L0_ffn_s_gate, dtype=np.float32),
        np.asarray(L0_ffn_wq_up, dtype=np.int32),
        np.asarray(L0_ffn_s_up, dtype=np.float32),
        np.asarray(L0_ffn_wq_down, dtype=np.int32),
        np.asarray(L0_ffn_s_down, dtype=np.float32),
    )
    return out[None, :, :]


# version 13
# speedup vs baseline: 1.0054x; 1.0054x over previous
import numpy as np
from contextlib import ExitStack

import concourse.bass as bass
import concourse.mybir as mybir
from concourse.bass_utils import run_bass_kernel_spmd

F32 = mybir.dt.float32
BF16 = mybir.dt.bfloat16
I32 = mybir.dt.int32
AF = mybir.ActivationFunctionType
OP = mybir.AluOpType

NCORES = 8
ENGINES = ("sync", "tensor", "vector", "scalar", "gpsimd")

CFG_REAL = dict(D=2048, H=5632, T=1024, TT=512)


class Prog:
    def __init__(self):
        self.ops = {e: [] for e in ENGINES}
        self.cnt = {}
        self.sems = {}

    def bump(self, sem, amt):
        self.cnt[sem] = self.cnt.get(sem, 0) + amt
        return self.cnt[sem]

    def wait(self, eng, sem, val):
        if val:
            self.ops[eng].append(lambda E, s=sem, v=val: E.wait_ge(self.sems[s], v))

    def op(self, eng, fn, inc=None):
        if inc is None:
            self.ops[eng].append(fn)
            return None
        sem, amt = inc
        tok = self.bump(sem, amt)
        self.ops[eng].append(
            lambda E, f=fn, s=sem, a=amt: f(E).then_inc(self.sems[s], a))
        return tok

    def dma(self, eng, out, in_, sem):
        return self.op(eng, lambda E, o=out, i=in_: E.dma_start(out=o, in_=i),
                       inc=(sem, 16))

    _DS = {"sync": ("ldw", "syn"), "gpsimd": ("gpd", "gpn")}

    def dsig(self, eng):
        dma_sem, tok_sem = self._DS[eng]
        self.wait(eng, dma_sem, self.cnt.get(dma_sem, 0))
        tok = self.bump(tok_sem, 1)
        self.ops[eng].append(
            lambda E, s=tok_sem: E.sem_inc(self.sems[s], 1))
        return tok

    def barrier(self):
        self.dsig("sync")
        self.dsig("gpsimd")
        s = dict(self.cnt)
        def w(eng, sem):
            self.wait(eng, sem, s.get(sem, 0))
        for eng in ENGINES:
            for sem in ("syn", "gpn", "ten", "vec", "sca"):
                w(eng, sem)


def build_program(cfg, debug=False):
    D, H, T, TT = cfg["D"], cfg["H"], cfg["T"], cfg["TT"]
    KC, HC = D // 128, H // 128
    ECC = KC
    NT = T // TT
    EPS = 1e-5

    nc = bass.Bass("TRN2", target_bir_lowering=False)

    x_h = nc.dram_tensor("x", [128, KC * T], F32, kind="ExternalInput").ap()
    wi_h = nc.dram_tensor("wi", [128, ECC * KC * 128], F32, kind="ExternalInput").ap()
    wf_h = nc.dram_tensor("wf", [128, ECC * KC * 128], F32, kind="ExternalInput").ap()
    wg_h = nc.dram_tensor("wg", [128, ECC * KC * 128], F32, kind="ExternalInput").ap()
    wo_h = nc.dram_tensor("wo", [128, KC * ECC * 128], F32, kind="ExternalInput").ap()
    wqg_h = nc.dram_tensor("wqg", [128, HC * KC * 128], I32, kind="ExternalInput").ap()
    wqu_h = nc.dram_tensor("wqu", [128, HC * KC * 128], I32, kind="ExternalInput").ap()
    wqd_h = nc.dram_tensor("wqd", [128, KC * HC * 128], I32, kind="ExternalInput").ap()
    gam_h = nc.dram_tensor("gamma", [128, ECC], F32, kind="ExternalInput").ap()
    sg_h = nc.dram_tensor("sg", [128, HC], F32, kind="ExternalInput").ap()
    su_h = nc.dram_tensor("su", [128, HC], F32, kind="ExternalInput").ap()
    sd_h = nc.dram_tensor("sd", [128, KC], F32, kind="ExternalInput").ap()
    rmask_h = nc.dram_tensor("rmask", [128, NCORES], F32, kind="ExternalInput").ap()
    rmm_h = nc.dram_tensor("rmm", [128, (NCORES - 1) * KC], F32,
                           kind="ExternalInput").ap()
    rmc_h = nc.dram_tensor("rmc", [128, (NCORES - 1) * KC], F32,
                           kind="ExternalInput").ap()
    out_h = nc.dram_tensor("out", [128, KC * T], F32, kind="ExternalOutput").ap()

    spk = "ExternalOutput" if debug else "Internal"
    d_o = nc.dram_tensor("d_o", [NT, 128, KC * TT], F32, kind=spk).ap()
    d_p = nc.dram_tensor("d_p", [NT, 128, KC * TT], F32, kind=spk).ap()
    d_yh = nc.dram_tensor("d_yh", [NT, 128, KC * TT], BF16, kind=spk).ap()
    d_yl = nc.dram_tensor("d_yl", [NT, 128, KC * TT], BF16, kind=spk).ap()
    d_x2 = nc.dram_tensor("d_x2", [NT, 128, KC * TT], F32, kind=spk).ap()
    d_cr = (nc.dram_tensor("d_cr", [128, KC], F32, kind="ExternalOutput").ap()
            if debug else None)
    d_gate = (nc.dram_tensor("d_gate", [NT, 128, KC * TT], F32,
                             kind="ExternalOutput").ap() if debug else None)
    d_iin = (nc.dram_tensor("d_iin", [NT, 128, KC * TT], F32,
                            kind="ExternalOutput").ap() if debug else None)
    d_init = (nc.dram_tensor("d_init", [NT, 128, 4 * KC], F32,
                             kind="ExternalOutput").ap() if debug else None)
    d_as = (nc.dram_tensor("d_as", [128, NCORES * 2 * KC], F32,
                           kind="ExternalOutput").ap() if debug else None)
    d_rm = (nc.dram_tensor("d_rm", [128, NCORES], F32,
                           kind="ExternalOutput").ap() if debug else None)
    d_vs = (nc.dram_tensor("d_vs", [128, NCORES * 2 * KC], F32,
                           kind="ExternalOutput").ap() if debug else None)
    d_h1 = (nc.dram_tensor("d_h1", [NT, 128, KC * TT], F32,
                           kind="ExternalOutput").ap() if debug else None)
    d_h2 = (nc.dram_tensor("d_h2", [NT, 128, KC * TT], F32,
                           kind="ExternalOutput").ap() if debug else None)
    cc_in = nc.dram_tensor("cc_in", [128, 2 * ECC], F32, kind="Internal").ap()
    cc_out = nc.dram_tensor("cc_out", [NCORES, 128, 2 * ECC], F32,
                            kind="Internal").ap()

    p = Prog()
    es = ExitStack()
    with es:
        SZ = KC * TT
        PAIR = KC * TT // 2
        W5 = 2 * KC * 128
        MISC = 11 * TT
        R7 = max(4 * (KC * 128 // 2), max(0, HC * TT - 2 * SZ))
        offs, o_ = {}, 0
        for name, sz in (("R0", SZ), ("R1", SZ), ("R2", SZ), ("R3", PAIR),
                         ("R4", PAIR), ("R5", W5), ("R6", MISC), ("R7", R7)):
            offs[name] = (o_, sz)
            o_ += sz
        mega = es.enter_context(nc.sbuf_tensor("mega", [128, o_], F32))

        def regf(name, off, n):
            a, sz = offs[name]
            assert off + n <= sz, (name, off, n, sz)
            return mega[:, a + off:a + off + n]

        def reg(name):
            return regf(name, 0, offs[name][1])

        def regb(name, off, n):
            a, sz = offs[name]
            assert off + n <= sz
            return mega[:, a + off:a + off + n].bitcast(BF16)

        def rch(name, c):
            return regf(name, c * TT, TT)

        def pch(name, c):
            return regb(name, c * TT // 2, TT // 2)

        hchunks = []
        for rn in ("R0", "R1", "R7"):
            a, sz = offs[rn]
            for off in range(0, sz - TT // 2 + 1, TT // 2):
                hchunks.append((rn, off))
        assert len(hchunks) >= 2 * HC, (len(hchunks), HC)

        def hhv(i):
            rn, off = hchunks[i]
            return regb(rn, off, TT // 2)

        def hlv(i):
            rn, off = hchunks[HC + i]
            return regb(rn, off, TT // 2)

        def msc(off, n):
            return regf("R6", off, n)

        scr_t = [msc(0, TT), msc(TT, TT)]
        scr_u = [msc(2 * TT, TT), msc(3 * TT, TT)]
        bm_t = msc(4 * TT, TT)
        br_t = msc(5 * TT, TT)
        bc_t = msc(6 * TT, TT)
        scr_v = [msc(7 * TT, TT), msc(8 * TT, TT)]
        scr_w = [msc(9 * TT, TT), msc(10 * TT, TT)]

        WBF = KC * 128 // 2

        def whiv(b):
            return regb("R7", (2 * b) * WBF, WBF)

        def wlov(b):
            return regb("R7", (2 * b + 1) * WBF, WBF)

        def wstv(b):
            return regf("R5", b * KC * 128, KC * 128)

        def ffnv(b):
            return regb("R5", b * WBF, WBF)

        DBF = HC * 128 // 2
        assert DBF <= PAIR

        def wdnv(b):
            return regb("R3" if b == 0 else "R4", 0, DBF)

        zeros = es.enter_context(nc.sbuf_tensor("zeros", [128, TT], F32))
        ones_c = es.enter_context(nc.sbuf_tensor("ones_c", [128, 1], F32))
        ones_r = es.enter_context(nc.sbuf_tensor("ones_r", [1, 128], F32))
        init_o = es.enter_context(nc.sbuf_tensor("init_o", [128, ECC], F32))
        init_p = es.enter_context(nc.sbuf_tensor("init_p", [128, ECC], F32))
        summ = es.enter_context(nc.sbuf_tensor("summ", [128, 2 * ECC], F32))
        asum = es.enter_context(nc.sbuf_tensor("asum", [128, NCORES * 2 * ECC], F32))
        carry = es.enter_context(nc.sbuf_tensor("carry", [128, ECC], F32))
        ctmp = es.enter_context(nc.sbuf_tensor("ctmp", [128, 3 * ECC], F32))
        gam_s = es.enter_context(nc.sbuf_tensor("gam_s", [128, ECC], F32))
        sg_s = es.enter_context(nc.sbuf_tensor("sg_s", [128, HC], F32))
        su_s = es.enter_context(nc.sbuf_tensor("su_s", [128, HC], F32))
        sd_s = es.enter_context(nc.sbuf_tensor("sd_s", [128, KC], F32))
        rm_s = es.enter_context(nc.sbuf_tensor("rm_s", [128, NCORES], F32))
        rmm_s = es.enter_context(nc.sbuf_tensor("rmm_s", [128, (NCORES - 1) * ECC], F32))
        rmc_s = es.enter_context(nc.sbuf_tensor("rmc_s", [128, (NCORES - 1) * ECC], F32))
        eps_s = es.enter_context(nc.sbuf_tensor("eps_s", [1, 1], F32))
        stv = es.enter_context(nc.sbuf_tensor("stv", [1, 6 * TT], F32))
        vsnap = (es.enter_context(nc.sbuf_tensor("vsnap",
                 [128, NCORES * 2 * ECC], F32)) if debug else None)

        def sv(i):
            return stv[0:1, i * TT:(i + 1) * TT]

        ps_full = [es.enter_context(nc.psum_tensor(f"ps{i}", [128, 512], F32))
                   for i in range(8)]
        ps = [t[:, 0:TT] for t in ps_full]

        sems = {}
        for s in ("ldw", "gpd", "ten", "vec", "sca", "cc", "syn", "gpn"):
            sems[s] = es.enter_context(nc.semaphore(s))
        p.sems = sems

        p.dma("sync", gam_s[:, :], gam_h, "ldw")
        p.dma("sync", sg_s[:, :], sg_h, "ldw")
        p.dma("sync", su_s[:, :], su_h, "ldw")
        p.dma("sync", sd_s[:, :], sd_h, "ldw")
        p.dma("sync", rm_s[:, :], rmask_h, "ldw")
        p.dma("sync", rmm_s[:, :], rmm_h, "ldw")
        p.dma("sync", rmc_s[:, :], rmc_h, "ldw")
        s_statics = p.dsig("sync")
        p.op("vector", lambda E: E.memset(zeros[:, :], 0.0))
        p.op("vector", lambda E: E.memset(ones_c[:, :], 1.0))
        p.op("vector", lambda E: E.memset(ones_r[:, :], 1.0))
        p.op("vector", lambda E: E.memset(init_o[:, :], 0.0))
        p.op("vector", lambda E: E.memset(eps_s[:, :], EPS))
        t_init = p.op("vector", lambda E: E.memset(init_p[:, :], 1.0),
                      inc=("vec", 1))

        def load_x(st, region, after=()):
            toks = {}
            for s_, v_ in after:
                p.wait("sync", s_, v_)
            for q in range(0, KC, 4):
                for kc in range(q, min(q + 4, KC)):
                    p.dma("sync", rch(region, kc),
                          x_h[:, kc * T + st * TT: kc * T + st * TT + TT],
                          "ldw")
                toks[q // 4] = p.dsig("sync")
            return toks

        def ln_stats(region, t_ready, rdy_sem):
            toks = t_ready if isinstance(t_ready, dict) else None

            def rdy(eng, kc):
                if toks is not None:
                    p.wait(eng, rdy_sem, toks[kc // 4])
                elif t_ready is not None:
                    p.wait(eng, rdy_sem, t_ready)
            p.wait("tensor", "vec", t_init)
            t_sum = None
            for kc in range(KC):
                rdy("tensor", kc)
                t_sum = p.op("tensor", lambda E, k=kc: E.matmul(
                    ps[5][0:1, :], ones_c[:, :], rch(region, k),
                    start=(k == 0), stop=(k == KC - 1)),
                    inc=("ten", 1) if kc == KC - 1 else None)
            sq_mm = {}
            t_sq = None
            for kc in range(KC):
                if kc >= 2:
                    p.wait("scalar", "ten", sq_mm[kc - 2])
                rdy("scalar", kc)
                tk = p.op("scalar", lambda E, k=kc: E.activation(
                    out=scr_u[k % 2][:, :], in_=rch(region, k), func=AF.Square),
                    inc=("sca", 1))
                p.wait("tensor", "sca", tk)
                t_sq = sq_mm[kc] = p.op("tensor", lambda E, k=kc: E.matmul(
                    ps[4][0:1, :], ones_c[:, :], scr_u[k % 2][:, :],
                    start=(k == 0), stop=(k == KC - 1)), inc=("ten", 1))
            p.wait("vector", "ten", max(t_sum, t_sq))
            wl0 = p.op("vector", lambda E: E.tensor_scalar(
                out=sv(0), in0=ps[5][0:1, :], scalar1=1.0 / D, scalar2=None,
                op0=OP.mult), inc=("vec", 1))
            p.wait("vector", "vec", wl0)
            wls = p.op("vector", lambda E: E.tensor_scalar(
                out=sv(1), in0=ps[4][0:1, :], scalar1=1.0 / D, scalar2=None,
                op0=OP.mult), inc=("vec", 1))
            p.wait("vector", "vec", wls)
            p.op("vector", lambda E: E.tensor_mul(sv(2), sv(0), sv(0)))
            t_var = p.op("vector", lambda E: E.tensor_sub(sv(3), sv(1), sv(2)),
                         inc=("vec", 1))
            t_r = _rsqrt_vec(sv(3), t_var, 1, 2)
            p.wait("tensor", "vec", t_r)
            p.op("tensor", lambda E: E.matmul(ps[6][:, :], ones_r[:, :], sv(0),
                                              start=True, stop=True))
            t_bc = p.op("tensor", lambda E: E.matmul(
                ps[7][:, :], ones_r[:, :], sv(5), start=True, stop=True),
                inc=("ten", 1))
            p.wait("vector", "ten", t_bc)
            p.op("vector", lambda E: E.tensor_copy(out=bm_t, in_=ps[6][:, :]))
            return p.op("vector", lambda E: E.tensor_copy(out=br_t, in_=ps[7][:, :]),
                        inc=("vec", 1))

        def _rsqrt_vec(v, t_v, ia, ib):
            a, b = sv(ia), sv(ib)
            w0 = p.op("vector", lambda E: E.tensor_scalar(
                out=sv(4), in0=v, scalar1=EPS, scalar2=None, op0=OP.add),
                inc=("vec", 1))
            p.wait("scalar", "vec", p.cnt["vec"])
            t_sqr = p.op("scalar", lambda E: E.activation(
                out=sv(5), in_=v, func=AF.Sqrt, bias=eps_s[0:1, 0:1]),
                inc=("sca", 1))
            p.wait("vector", "sca", t_sqr)
            p.wait("vector", "vec", w0)
            p.op("vector", lambda E: E.reciprocal(a, sv(5)))
            p.op("vector", lambda E: E.tensor_mul(b, a, a))
            p.op("vector", lambda E: E.tensor_mul(sv(5), b, sv(4)))
            w1 = p.op("vector", lambda E: E.tensor_scalar(
                out=b, in0=sv(5), scalar1=-0.5, scalar2=1.5,
                op0=OP.mult, op1=OP.add), inc=("vec", 1))
            p.wait("vector", "vec", w1)
            return p.op("vector", lambda E: E.tensor_mul(sv(5), a, b),
                        inc=("vec", 1))

        import os as _os
        _noln = bool(_os.environ.get("KDBG_NOLN"))

        def ln_apply_split(region, t_stats):
            p.wait("vector", "vec", t_stats)
            tk = None
            for kc in range(KC):
                s = scr_t[kc % 2]
                if not _noln:
                    p.op("vector", lambda E, k=kc, s=s: E.tensor_sub(
                        s, rch(region, k), bm_t))
                    p.op("vector", lambda E, k=kc, s=s: E.tensor_mul(
                        rch(region, k), s, br_t))
                p.op("vector", lambda E, k=kc: E.tensor_copy(
                    out=pch("R3", k), in_=rch(region, k)))
                tk = p.op("vector", lambda E, k=kc: E.tensor_tensor(
                    out=pch("R4", k), in0=rch(region, k), in1=pch("R3", k),
                    op=OP.subtract), inc=("vec", 1))
            return tk

        def projection(w_h, n_out, nk, rhs_hi, rhs_lo, evac, rhs_waits,
                       first_waits=None):
            snap0 = dict(p.cnt)
            state = {}

            def stage(oc):
                b = oc % 2
                wst, whi, wlo = wstv(b), whiv(b), wlov(b)
                prev = state.get(("split", b))
                if prev is not None:
                    p.wait("sync", "vec", prev)
                elif first_waits is not None:
                    for s_, v_ in first_waits["stage"]:
                        p.wait("sync", s_, v_)
                else:
                    p.wait("sync", "vec", snap0.get("vec", 0))
                    p.wait("sync", "sca", snap0.get("sca", 0))
                p.dma("sync", wst,
                      w_h[:, oc * nk * 128:(oc + 1) * nk * 128], "ldw")
                s_w = p.dsig("sync")
                tb = state.get(("used", b))
                if tb is None:
                    tb = (first_waits or {}).get("ten", snap0.get("ten", 0))
                p.wait("vector", "syn", s_w)
                p.wait("vector", "ten", tb)
                p.op("vector", lambda E, w=wst, h=whi: E.tensor_copy(
                    out=h, in_=w))
                t_lo = p.op("vector", lambda E, w=wst, h=whi, l=wlo:
                            E.tensor_tensor(out=l, in0=w, in1=h,
                                            op=OP.subtract), inc=("vec", 1))
                state[("split", b)] = t_lo
                state[("ready", oc)] = t_lo

            stage(0)
            if n_out > 1:
                stage(1)
            for oc in range(n_out):
                b = oc % 2
                whi, wlo = whiv(b), wlov(b)
                p.wait("tensor", "vec", state[("ready", oc)])
                if oc == 0:
                    for s_, v_ in rhs_waits:
                        p.wait("tensor", s_, v_)
                bf = state.get(("bank", b))
                if bf is not None:
                    p.wait("tensor", bf[0], bf[1])
                elif first_waits is not None:
                    for s_, v_ in first_waits["bank"]:
                        p.wait("tensor", s_, v_)
                else:
                    p.wait("tensor", "sca", snap0.get("sca", 0))
                    p.wait("tensor", "vec", snap0.get("vec", 0))
                t_grp = None
                pb = ps[b]
                for kc in range(nk):
                    wh = whi[:, kc * 128:(kc + 1) * 128]
                    wl = wlo[:, kc * 128:(kc + 1) * 128]
                    p.op("tensor", lambda E, a=pb, w=wh, r=rhs_hi(kc), st=(kc == 0):
                         E.matmul(a[:, :], w, r, start=st, stop=False))
                    p.op("tensor", lambda E, a=pb, w=wh, r=rhs_lo(kc):
                         E.matmul(a[:, :], w, r, start=False, stop=False))
                    t_grp = p.op(
                        "tensor",
                        lambda E, a=pb, w=wl, r=rhs_hi(kc), sp=(kc == nk - 1):
                        E.matmul(a[:, :], w, r, start=False, stop=sp),
                        inc=("ten", 1) if kc == nk - 1 else None)
                state[("used", b)] = t_grp
                state[("bank", b)] = evac(oc, ps[b][:, :], t_grp)
                if oc + 2 < n_out:
                    stage(oc + 2)
            return state

        spill_tok = {}
        prev_i = None
        for st in range(NT):
            if st == 0:
                p.barrier()
            t_x = load_x(st, "R0",
                         after=([("gpn", spill_tok[("o", st - 1)])]
                                if st > 0 else ()))
            if _os.environ.get("KDBG_NOSTATS"):
                p.wait("vector", "syn", max(t_x.values()))
                t_st = p.op("vector", lambda E: E.engine_nop(), inc=("vec", 1))
            else:
                t_st = ln_stats("R0", t_x, "syn")
            if st > 0:
                p.wait("vector", "gpn", spill_tok[("rest", st - 1)])
                p.wait("scalar", "gpn", spill_tok[("rest", st - 1)])
            t_pair = ln_apply_split("R0", t_st)

            evF = {}

            def evac_f(oc, psap, t_grp):
                p.wait("scalar", "ten", t_grp)
                import os as _os
                if _os.environ.get("KDBG_NOSIG"):
                    tk = p.op("scalar", lambda E, o=oc, a=psap: E.copy(
                        out=rch("R1", o), in_=a), inc=("sca", 1))
                else:
                    tk = p.op("scalar", lambda E, o=oc, a=psap: E.activation(
                        out=rch("R1", o), in_=a, func=AF.Sigmoid), inc=("sca", 1))
                evF[oc] = tk
                return ("sca", tk)

            if prev_i is None:
                f_fw = {"stage": [], "bank": [], "ten": 0}
            else:
                f_fw = {"stage": [("vec", prev_i[("ready", ECC - 1)])],
                        "bank": [prev_i[("bank", 0)], prev_i[("bank", 1)]],
                        "ten": max(prev_i[("used", 0)], prev_i[("used", 1)])}
            f_state = projection(wf_h, ECC, KC, lambda kc: pch("R3", kc),
                                 lambda kc: pch("R4", kc), evac_f,
                                 rhs_waits=[("vec", t_pair)],
                                 first_waits=f_fw)

            evI = {}
            t_scan_tok = {}

            def evac_i(oc, psap, t_grp):
                p.wait("scalar", "ten", t_grp)
                if oc >= 2:
                    p.wait("scalar", "vec", evI[oc - 2])
                t_si = p.op("scalar", lambda E, o=oc, a=psap: E.activation(
                    out=scr_u[o % 2][:, :], in_=a, func=AF.Silu), inc=("sca", 1))
                p.wait("vector", "sca", t_si)
                tk = p.op("vector", lambda E, o=oc: E.scalar_tensor_tensor(
                    out=rch("R2", o), in0=rch("R1", o), scalar=1.0,
                    in1=scr_u[o % 2][:, :], op0=OP.subtract, op1=OP.mult),
                    inc=("vec", 1))
                evI[oc] = tk
                p.wait("vector", "sca", evF[oc])
                p.wait("vector", "vec", tk)
                tso = p.op("vector", lambda E, e=oc: E.tensor_tensor_scan(
                    out=rch("R0", e), data0=rch("R1", e), data1=rch("R2", e),
                    initial=init_o[:, e:e + 1], op0=OP.mult, op1=OP.subtract),
                    inc=("vec", 1))
                p.wait("vector", "vec", tso)
                p.op("vector", lambda E, e=oc: E.tensor_copy(
                    out=init_o[:, e:e + 1], in_=rch("R0", e)[:, TT - 1:TT]))
                tsp = p.op("vector", lambda E, e=oc: E.tensor_tensor_scan(
                    out=rch("R2", e), data0=rch("R1", e), data1=zeros[:, :],
                    initial=init_p[:, e:e + 1], op0=OP.mult, op1=OP.add),
                    inc=("vec", 1))
                p.wait("vector", "vec", tsp)
                t_scan_tok[oc] = p.op("vector", lambda E, e=oc: E.tensor_copy(
                    out=init_p[:, e:e + 1], in_=rch("R2", e)[:, TT - 1:TT]),
                    inc=("vec", 1))
                return ("sca", t_si)

            prev_i = projection(
                wi_h, ECC, KC, lambda kc: pch("R3", kc),
                lambda kc: pch("R4", kc), evac_i,
                rhs_waits=[("vec", t_pair)],
                first_waits={
                    "stage": [("vec", f_state[("ready", ECC - 1)])],
                    "bank": [f_state[("bank", 0)], f_state[("bank", 1)]],
                    "ten": max(f_state[("used", 0)], f_state[("used", 1)])})
            t_scan = t_scan_tok[ECC - 1]

            if st < NT - 1:
                for q in range(0, ECC, 4):
                    hi = min(q + 4, ECC) - 1
                    p.wait("gpsimd", "vec", t_scan_tok[hi])
                    p.dma("gpsimd", d_o[st][:, q * TT:(hi + 1) * TT],
                          regf("R0", q * TT, (hi + 1 - q) * TT), "gpd")
                spill_tok[("o", st)] = p.dsig("gpsimd")
                p.dma("gpsimd", d_p[st], reg("R2"), "gpd")
                p.dma("gpsimd", d_yh[st], reg("R3").bitcast(BF16), "gpd")
                p.dma("gpsimd", d_yl[st], reg("R4").bitcast(BF16), "gpd")
                spill_tok[("rest", st)] = p.dsig("gpsimd")

        p.wait("vector", "vec", t_scan)
        p.op("vector", lambda E: E.tensor_copy(out=summ[:, 0:ECC],
                                               in_=init_p[:, :]))
        t_summ = p.op("vector", lambda E: E.tensor_copy(
            out=summ[:, ECC:2 * ECC], in_=init_o[:, :]), inc=("vec", 1))
        p.barrier()
        snapP2 = dict(p.cnt)
        p.wait("gpsimd", "vec", t_summ)
        p.dma("gpsimd", cc_in, summ[:, :], "gpd")
        p.wait("gpsimd", "gpd", p.cnt["gpd"])
        t_cc = p.op("gpsimd", lambda E: E.collective_compute(
            "AllGather", OP.bypass, replica_groups=[list(range(NCORES))],
            ins=[cc_in], outs=[cc_out]), inc=("cc", 1))
        p.wait("gpsimd", "cc", t_cc)
        for r in range(NCORES):
            p.dma("gpsimd", asum[:, r * 2 * ECC:(r + 1) * 2 * ECC],
                  cc_out[r], "gpd")
        g_as = p.dsig("gpsimd")
        if debug:
            p.dma("gpsimd", d_as, asum[:, :], "gpd")
            p.dma("gpsimd", d_rm, rm_s[:, :], "gpd")

        def gop(fn):
            tok = p.op("gpsimd", fn, inc=("gpn", 1))
            p.wait("gpsimd", "gpn", tok)
            return tok
        gop(lambda E: E.memset(carry[:, :], 0.0))
        for r in range(NCORES - 1):
            A_r = asum[:, r * 2 * ECC: r * 2 * ECC + ECC]
            B_r = asum[:, r * 2 * ECC + ECC: (r + 1) * 2 * ECC]
            mm_r = rmm_s[:, r * ECC:(r + 1) * ECC]
            mc_r = rmc_s[:, r * ECC:(r + 1) * ECC]
            tA, tB, tC = (ctmp[:, 0:ECC], ctmp[:, ECC:2 * ECC],
                          ctmp[:, 2 * ECC:3 * ECC])
            gop(lambda E, A=A_r, m=mm_r, t=tC: E.tensor_mul(t, A, m))
            gop(lambda E, t=tA, c=tC, m=mc_r: E.tensor_add(t, c, m))
            gop(lambda E, B=B_r, m=mm_r, t=tB: E.tensor_mul(t, B, m))
            gop(lambda E, t=tC, a=tA: E.tensor_mul(t, carry[:, :], a))
            gop(lambda E, t=tC, b=tB: E.tensor_add(carry[:, :], t, b))
        t_carry = gop(lambda E: E.tensor_copy(out=ctmp[:, 0:ECC],
                                              in_=carry[:, :]))
        if debug:
            p.dma("gpsimd", d_cr, carry[:, :], "gpd")

        def prh(rn, ec):
            return regf(rn, ec * TT, TT // 2).bitcast(BF16)

        def prl(rn, ec):
            return regf(rn, ec * TT + TT // 2, TT // 2).bitcast(BF16)

        assert NT == 2
        P2MAPS = {1: dict(o="R0", pp="R2", g="R1"),
                  0: dict(o="R2", pp="R1", g="R0")}

        def p2_body(st, first, rd_toks, extra_gwait, prefetch,
                    g_first_waits=None):
            Ro, Rp, Rg = (P2MAPS[st][k] for k in ("o", "pp", "g"))
            t_rdp, t_rd = rd_toks if rd_toks else (None, None)
            if first:
                for eng in ENGINES:
                    for sem in ("syn", "gpn", "ten", "vec", "sca"):
                        p.wait(eng, sem, snapP2.get(sem, 0))
            else:
                p.barrier()

            g2mm = {}
            g2sq = {}
            t_sil = {}
            t_oa_tok = {}
            DLY = min(3, ECC)

            def emit_g2(o):
                p.wait("tensor", "sca", g2sq[o])
                g2mm[o] = p.op("tensor", lambda E, o=o: E.matmul(
                    ps[4][0:1, :], ones_c[:, :], scr_u[o % 2][:, :],
                    start=(o == 0), stop=(o == ECC - 1)), inc=("ten", 1))

            def chain(ec):
                u = scr_w[ec % 2]
                if ec == 0:
                    p.wait("vector", "gpn", t_carry)
                    if t_rd is not None:
                        p.wait("vector", "gpn", t_rd)
                if ec >= 2:
                    p.wait("vector", "sca", t_sil[ec - 2])
                tf = p.op("vector", lambda E, e=ec, u=u: E.scalar_tensor_tensor(
                    out=u, in0=rch(Rp, e), scalar=carry[:, e:e + 1],
                    in1=rch(Ro, e), op0=OP.mult, op1=OP.add), inc=("vec", 1))
                p.wait("scalar", "vec", tf)
                t_sil[ec] = p.op("scalar", lambda E, e=ec, u=u: E.activation(
                    out=rch(Rp, e), in_=u, func=AF.Silu), inc=("sca", 1))
                p.wait("vector", "sca", t_sil[ec])
                p.op("vector", lambda E, e=ec: E.tensor_mul(
                    rch(Ro, e), rch(Rg, e), rch(Rp, e)))
                whi_ = p.op("vector", lambda E, e=ec: E.tensor_scalar(
                    out=prh(Rp, e), in0=rch(Ro, e),
                    scalar1=gam_s[:, e:e + 1], scalar2=None, op0=OP.mult),
                    inc=("vec", 1))
                p.wait("vector", "vec", whi_)
                t_oa_tok[ec] = p.op("vector", lambda E, e=ec: E.scalar_tensor_tensor(
                    out=prl(Rp, e), in0=rch(Ro, e),
                    scalar=gam_s[:, e:e + 1], in1=prh(Rp, e),
                    op0=OP.mult, op1=OP.subtract), inc=("vec", 1))

            def evac_g(oc, psap, t_grp):
                p.wait("vector", "ten", t_grp)
                if oc == 0 and extra_gwait is not None:
                    p.wait("vector", "gpn", extra_gwait)
                t_cp = p.op("vector", lambda E, o=oc, a=psap: E.tensor_copy(
                    out=rch(Rg, o), in_=a), inc=("vec", 1))
                if oc >= 2:
                    p.wait("scalar", "ten", g2mm[oc - 2])
                p.wait("scalar", "vec", t_cp)
                g2sq[oc] = p.op("scalar", lambda E, o=oc: E.activation(
                    out=scr_u[o % 2][:, :], in_=rch(Rg, o), func=AF.Square),
                    inc=("sca", 1))
                if oc >= 1:
                    emit_g2(oc - 1)
                if oc >= DLY:
                    chain(oc - DLY)
                return ("vec", t_cp)

            gw = [("gpn", t_rdp)] if t_rdp is not None else []
            projection(wg_h, ECC, KC, lambda kc: pch("R3", kc),
                       lambda kc: pch("R4", kc), evac_g, rhs_waits=gw,
                       first_waits=g_first_waits)
            t_glast = p.cnt["ten"]
            if prefetch is not None:
                prefetch("after_g", ten=t_glast)
            emit_g2(ECC - 1)
            p.wait("vector", "ten", g2mm[ECC - 1])
            t_ms = p.op("vector", lambda E: E.tensor_scalar(
                out=sv(1), in0=ps[4][0:1, :], scalar1=1.0 / D, scalar2=None,
                op0=OP.mult), inc=("vec", 1))
            t_r2 = _rsqrt_vec(sv(1), t_ms, 2, 3)
            p.wait("tensor", "vec", t_r2)
            t_bc2 = p.op("tensor", lambda E: E.matmul(
                ps[6][:, :], ones_r[:, :], sv(5), start=True, stop=True),
                inc=("ten", 1))
            p.wait("vector", "ten", t_bc2)
            t_brd = p.op("vector", lambda E: E.tensor_copy(
                out=br_t, in_=ps[6][:, :]), inc=("vec", 1))

            for ec in range(max(0, ECC - DLY), ECC):
                chain(ec)
            t_oa = t_oa_tok[ECC - 1]

            p.wait("gpsimd", "vec", t_oa)
            for kc in range(KC):
                p.dma("gpsimd", rch(Rg, kc),
                      x_h[:, kc * T + st * TT: kc * T + st * TT + TT], "gpd")
            t_xr = p.dsig("gpsimd")

            ew_tok = {}

            def evac_wo(oc, psap, t_grp):
                p.wait("vector", "ten", t_grp)
                if oc == 0:
                    p.wait("vector", "gpn", t_xr)
                    p.wait("vector", "vec", t_brd)
                s = scr_t[oc % 2]
                p.op("vector", lambda E, o=oc, a=psap, s=s: E.tensor_mul(
                    s, a, br_t))
                tk = p.op("vector", lambda E, o=oc, s=s: E.tensor_add(
                    rch(Ro, o), s, rch(Rg, o)), inc=("vec", 1))
                ew_tok[oc] = tk
                return ("vec", tk)

            wo_state = projection(wo_h, KC, ECC, lambda ec: prh(Rp, ec),
                                  lambda ec: prl(Rp, ec), evac_wo,
                                  rhs_waits=[("vec", t_oa)])
            t_wolast_ten = p.cnt["ten"]
            t_wolast_vec = p.cnt["vec"]
            if prefetch is not None:
                prefetch("after_wo", ten=t_wolast_ten, vec=t_wolast_vec)
            return t_wolast_vec, ew_tok, wo_state

        pf_toks = {}

        def prefetch_st0(stage_name, ten=None, vec=None):
            if stage_name == "after_g":
                p.wait("gpsimd", "ten", ten)
                p.dma("gpsimd", reg("R3").bitcast(BF16), d_yh[0], "gpd")
                p.dma("gpsimd", reg("R4").bitcast(BF16), d_yl[0], "gpd")
                pf_toks["rdp"] = p.dsig("gpsimd")
            else:
                p.wait("gpsimd", "ten", ten)
                p.dma("gpsimd", reg("R2"), d_o[0], "gpd")
                p.wait("gpsimd", "vec", vec)
                p.dma("gpsimd", reg("R1"), d_p[0], "gpd")
                pf_toks["rd"] = p.dsig("gpsimd")

        t_x2v, ew1, wo1 = p2_body(NT - 1, True, None, None, prefetch_st0)
        p.wait("gpsimd", "vec", t_x2v)
        p.dma("gpsimd", d_x2[1], reg("R0"), "gpd")
        t_x2sp = p.dsig("gpsimd")

        tight = {"stage": [("vec", wo1[("ready", KC - 1)])],
                 "bank": [("vec", ew1[KC - 1])]}
        _, ew0, _ = p2_body(0, False, (pf_toks["rdp"], pf_toks["rd"]),
                            t_x2sp, None, g_first_waits=tight)

        prev_ev = [None]
        out_tok_prev = [None]
        for st in range(NT):
            if st > 0:
                ev0 = prev_ev[0]
                snapP3b = dict(p.cnt)
                for eng in ("gpsimd", "scalar", "tensor", "vector"):
                    for sem in ("ten", "vec", "sca"):
                        p.wait(eng, sem, snapP3b.get(sem, 0))
                p.wait("sync", "gpn", snapP3b.get("gpn", 0))
                rd_toks3 = {}
                for q in range(0, KC, 4):
                    hi = min(q + 4, KC) - 1
                    p.wait("sync", "vec", ev0[hi])
                    p.wait("sync", "syn", out_tok_prev[0][hi])
                    p.dma("sync", d_x2[st][:, q * TT:(hi + 1) * TT] if False
                          else reg("R2")[:, q * TT:(hi + 1) * TT],
                          d_x2[st][:, q * TT:(hi + 1) * TT], "ldw")
                    rd_toks3[q // 4] = p.dsig("sync")
                t_st3 = ln_stats("R2", rd_toks3, "syn")
            else:
                snapP3 = dict(p.cnt)
                p.wait("gpsimd", "sca", snapP3.get("sca", 0))
                p.wait("gpsimd", "ten", snapP3.get("ten", 0))
                p.wait("gpsimd", "vec", snapP3.get("vec", 0))
                ew_batches = {q: ew0[min(q * 4 + 3, KC - 1)]
                              for q in range((KC + 3) // 4)}
                t_st3 = ln_stats("R2", ew_batches, "vec")
            p.wait("vector", "vec", t_st3)
            t_p3 = None
            for kc in range(KC):
                s, u = scr_t[kc % 2], scr_w[kc % 2]
                p.op("vector", lambda E, k=kc, s=s: E.tensor_sub(
                    s, rch("R2", k), bm_t))
                p.op("vector", lambda E, s=s, u=u: E.tensor_mul(u, s, br_t))
                p.op("vector", lambda E, k=kc, u=u: E.tensor_copy(
                    out=pch("R3", k), in_=u))
                t_p3 = p.op("vector", lambda E, k=kc, u=u: E.tensor_tensor(
                    out=pch("R4", k), in0=u, in1=pch("R3", k), op=OP.subtract),
                    inc=("vec", 1))

            wu_used = {}
            sg_tok = {}
            hf_tok = {}
            cs_tok = {}
            t_h = {}
            gbank = {}
            ubank = {}

            def emit_cs(h):
                p.wait("tensor", "vec", t_h[h])
                cs_tok[h] = p.op("tensor", lambda E, h=h, b=scr_w[h % 2]:
                                 E.matmul(ps[4][0:1, :], ones_c[:, :], b,
                                          start=(h == 0), stop=(h == HC - 1)),
                                 inc=("ten", 1))

            for hc in range(HC):
                bg, bu = hc % 2, 2 + hc % 2
                tb = wu_used.get(bg)
                if tb is not None:
                    p.wait("gpsimd", "ten", tb)
                p.dma("gpsimd", ffnv(bg),
                      wqg_h[:, hc * KC * 128:(hc + 1) * KC * 128], "gpd")
                p.dma("gpsimd", ffnv(2 + bg),
                      wqu_h[:, hc * KC * 128:(hc + 1) * KC * 128], "gpd")
                t_w2 = p.dsig("gpsimd")
                p.wait("tensor", "gpn", t_w2)
                if hc == 0:
                    p.wait("tensor", "vec", t_p3)
                bf = gbank.get(bg)
                if bf is not None:
                    p.wait("tensor", "sca", bf)
                t_gg = None
                for kc in range(KC):
                    w = ffnv(bg)[:, kc * 128:(kc + 1) * 128]
                    p.op("tensor", lambda E, a=ps[bg], w=w, k=kc: E.matmul(
                        a[:, :], w, pch("R3", k), start=(k == 0),
                        stop=False))
                    t_gg = p.op("tensor", lambda E, a=ps[bg], w=w, k=kc: E.matmul(
                        a[:, :], w, pch("R4", k), start=False,
                        stop=(k == KC - 1)),
                        inc=("ten", 1) if kc == KC - 1 else None)
                p.wait("scalar", "ten", t_gg)
                if hc >= 2:
                    p.wait("scalar", "vec", hf_tok[hc - 2])
                t_sg2 = p.op("scalar", lambda E, h=hc, b=bg: E.activation(
                    out=scr_u[h % 2][:, :], in_=ps[b][:, :], func=AF.Silu,
                    scale=sg_s[:, h:h + 1]), inc=("sca", 1))
                sg_tok[hc] = t_sg2
                gbank[bg] = t_sg2
                bf = ubank.get(bu)
                if bf is not None:
                    p.wait("tensor", "vec", bf)
                t_ug = None
                for kc in range(KC):
                    w = ffnv(2 + bg)[:, kc * 128:(kc + 1) * 128]
                    p.op("tensor", lambda E, a=ps[bu], w=w, k=kc: E.matmul(
                        a[:, :], w, pch("R3", k), start=(k == 0),
                        stop=False))
                    t_ug = p.op("tensor", lambda E, a=ps[bu], w=w, k=kc: E.matmul(
                        a[:, :], w, pch("R4", k), start=False,
                        stop=(k == KC - 1)),
                        inc=("ten", 1) if kc == KC - 1 else None)
                wu_used[bg] = t_ug
                p.wait("vector", "ten", t_ug)
                p.wait("vector", "sca", t_sg2)
                if hc >= 2:
                    p.wait("vector", "ten", cs_tok[hc - 2])
                hbuf = scr_w[hc % 2]
                t_hf = p.op("vector",
                            lambda E, h=hc, b=hbuf, u=bu: E.scalar_tensor_tensor(
                                out=b, in0=ps[u][:, :], scalar=su_s[:, h:h + 1],
                                in1=scr_u[h % 2][:, :], op0=OP.mult, op1=OP.mult),
                            inc=("vec", 1))
                hf_tok[hc] = t_hf
                ubank[bu] = t_hf
                p.wait("vector", "vec", t_hf)
                p.op("vector", lambda E, h=hc, b=hbuf: E.tensor_copy(
                    out=hhv(h), in_=b))
                t_h[hc] = p.op("vector", lambda E, h=hc, b=hbuf: E.tensor_tensor(
                    out=hlv(h), in0=b, in1=hhv(h), op=OP.subtract),
                    inc=("vec", 1))
                if hc >= 1:
                    emit_cs(hc - 1)

            emit_cs(HC - 1)
            t_cs = cs_tok[HC - 1]
            p.wait("vector", "ten", t_cs)
            t_cs2 = p.op("vector", lambda E: E.tensor_scalar(
                out=sv(0), in0=ps[4][0:1, :], scalar1=128.0, scalar2=None,
                op0=OP.mult), inc=("vec", 1))
            p.wait("tensor", "vec", t_cs2)
            t_bc3 = p.op("tensor", lambda E: E.matmul(
                ps[6][:, :], ones_r[:, :], sv(0), start=True, stop=True),
                inc=("ten", 1))
            p.wait("vector", "ten", t_bc3)
            t_bcd = p.op("vector", lambda E: E.tensor_copy(
                out=bc_t, in_=ps[6][:, :]), inc=("vec", 1))

            p.wait("tensor", "sca", p.cnt["sca"])
            p.wait("tensor", "vec", p.cnt["vec"])
            dn_used = {}
            dbank = {}
            out_tok = {}
            out_tok_prev[0] = out_tok
            ev_tok = prev_ev[0] = {}
            for dc in range(KC):
                b = dc % 2
                tb = dn_used.get(b)
                if tb is not None:
                    p.wait("gpsimd", "ten", tb)
                if dc < 2:
                    p.wait("gpsimd", "ten", t_cs)
                p.dma("gpsimd", wdnv(b),
                      wqd_h[:, dc * HC * 128:(dc + 1) * HC * 128], "gpd")
                t_wd = p.dsig("gpsimd")
                p.wait("tensor", "gpn", t_wd)
                bf = dbank.get(b)
                if bf is not None:
                    p.wait("tensor", "vec", bf)
                t_dg = None
                for hc in range(HC):
                    w = wdnv(b)[:, hc * 128:(hc + 1) * 128]
                    p.op("tensor", lambda E, a=ps[b], w=w, h=hc: E.matmul(
                        a[:, :], w, hhv(h), start=(h == 0), stop=False))
                    t_dg = p.op("tensor", lambda E, a=ps[b], w=w, h=hc: E.matmul(
                        a[:, :], w, hlv(h), start=False,
                        stop=(h == HC - 1)),
                        inc=("ten", 1) if hc == HC - 1 else None)
                dn_used[b] = t_dg
                p.wait("vector", "ten", t_dg)
                p.wait("vector", "vec", t_bcd)
                if dc >= 2:
                    p.wait("vector", "syn", out_tok[dc - 2])
                ob = scr_v[dc % 2]
                p.op("vector", lambda E, a=ps[b][:, :], o=ob: E.tensor_sub(
                    o, a, bc_t))
                t_ev = p.op("vector", lambda E, o=ob, d=dc: E.scalar_tensor_tensor(
                    out=o, in0=o, scalar=sd_s[:, d:d + 1], in1=rch("R2", d),
                    op0=OP.mult, op1=OP.add), inc=("vec", 1))
                dbank[b] = t_ev
                ev_tok[dc] = t_ev
                p.wait("sync", "vec", t_ev)
                p.dma("sync", out_h[:, dc * T + st * TT: dc * T + st * TT + TT],
                      ob, "ldw")
                out_tok[dc] = p.dsig("sync")

        p.wait("sync", "ldw", p.cnt["ldw"])
        p.wait("gpsimd", "gpd", p.cnt["gpd"])

        block = es.enter_context(nc.Block())

        def emit(name):
            def body(E):
                for fn in p.ops[name]:
                    fn(E)
            return body

        block.sync(emit("sync"))
        block.tensor(emit("tensor"))
        block.vector(emit("vector"))
        block.scalar(emit("scalar"))
        block.gpsimd(emit("gpsimd"))

    return nc



def _img_act(xs, T, KC):
    return np.ascontiguousarray(
        xs.reshape(T, KC, 128).transpose(2, 1, 0).reshape(128, KC * T))


def _img_w(WT, n_in_c, n_out_c):
    r = WT.reshape(n_in_c, 128, n_out_c, 128).transpose(1, 2, 0, 3)
    return np.ascontiguousarray(r.reshape(128, n_out_c * n_in_c * 128))


def _img_vec(v, nchunks):
    return np.ascontiguousarray(v.reshape(nchunks, 128).T)


def run(cfg, x, Wi, Wf, Wg, gam, Wo, wqg, sgv, wqu, suv, wqd, sdv,
        debug=False):
    D, H, T = cfg["D"], cfg["H"], cfg["T"]
    KC, HC = D // 128, H // 128
    wi_img = _img_w(np.ascontiguousarray(Wi.T), KC, KC)
    wf_img = _img_w(np.ascontiguousarray(Wf.T), KC, KC)
    wg_img = _img_w(np.ascontiguousarray(Wg.T), KC, KC)
    wo_img = _img_w(np.ascontiguousarray(Wo.T), KC, KC)
    wqg_img = _img_w(np.ascontiguousarray(wqg.T), KC, HC)
    wqu_img = _img_w(np.ascontiguousarray(wqu.T), KC, HC)
    wqd_img = _img_w(np.ascontiguousarray(wqd.T), HC, KC)
    gam_img = _img_vec(gam, KC)
    sg_img = _img_vec(sgv, HC)
    su_img = _img_vec(suv, HC)
    sd_img = _img_vec(sdv, KC)

    nc = build_program(cfg, debug=debug)
    in_maps = []
    for k in range(NCORES):
        xs = x[k * T:(k + 1) * T]
        rm = np.tile((np.arange(NCORES) < k).astype(np.float32), (128, 1))
        mvec = (np.arange(NCORES - 1) < k).astype(np.float32)
        rmm = np.tile(np.repeat(mvec, KC), (128, 1)).astype(np.float32)
        rmc = np.tile(np.repeat(1.0 - mvec, KC), (128, 1)).astype(np.float32)
        in_maps.append({
            "x": _img_act(xs, T, KC),
            "wi": wi_img, "wf": wf_img, "wg": wg_img, "wo": wo_img,
            "wqg": wqg_img, "wqu": wqu_img, "wqd": wqd_img,
            "gamma": gam_img, "sg": sg_img, "su": su_img, "sd": sd_img,
            "rmask": np.ascontiguousarray(rm),
            "rmm": np.ascontiguousarray(rmm),
            "rmc": np.ascontiguousarray(rmc),
        })
    res = run_bass_kernel_spmd(nc, in_maps, core_ids=list(range(NCORES)))
    outs = []
    for k in range(NCORES):
        o = np.asarray(res.results[k]["out"])
        outs.append(o.reshape(128, KC, T).transpose(2, 1, 0).reshape(T, D))
    full = np.concatenate(outs, axis=0)
    if debug:
        return full, res.results
    return full


def kernel(L0_attn_hidden_states, L0_attn_Wi, L0_attn_Wf, L0_attn_Wg,
           L0_attn_g_norm_weight, L0_attn_Wo, L0_ffn_wq_gate, L0_ffn_s_gate,
           L0_ffn_wq_up, L0_ffn_s_up, L0_ffn_wq_down, L0_ffn_s_down):
    cfg = CFG_REAL
    out = run(
        cfg,
        np.asarray(L0_attn_hidden_states, dtype=np.float32)[0],
        np.asarray(L0_attn_Wi, dtype=np.float32),
        np.asarray(L0_attn_Wf, dtype=np.float32),
        np.asarray(L0_attn_Wg, dtype=np.float32),
        np.asarray(L0_attn_g_norm_weight, dtype=np.float32),
        np.asarray(L0_attn_Wo, dtype=np.float32),
        np.asarray(L0_ffn_wq_gate, dtype=np.int32),
        np.asarray(L0_ffn_s_gate, dtype=np.float32),
        np.asarray(L0_ffn_wq_up, dtype=np.int32),
        np.asarray(L0_ffn_s_up, dtype=np.float32),
        np.asarray(L0_ffn_wq_down, dtype=np.int32),
        np.asarray(L0_ffn_s_down, dtype=np.float32),
    )
    return out[None, :, :]


# version 14 (final)
# speedup vs baseline: 1.0161x; 1.0107x over previous
import numpy as np
from contextlib import ExitStack

import concourse.bass as bass
import concourse.mybir as mybir
from concourse.bass_utils import run_bass_kernel_spmd

F32 = mybir.dt.float32
BF16 = mybir.dt.bfloat16
I32 = mybir.dt.int32
AF = mybir.ActivationFunctionType
OP = mybir.AluOpType

NCORES = 8
ENGINES = ("sync", "tensor", "vector", "scalar", "gpsimd")

CFG_REAL = dict(D=2048, H=5632, T=1024, TT=512)


class Prog:
    def __init__(self):
        self.ops = {e: [] for e in ENGINES}
        self.cnt = {}
        self.sems = {}

    def bump(self, sem, amt):
        self.cnt[sem] = self.cnt.get(sem, 0) + amt
        return self.cnt[sem]

    def wait(self, eng, sem, val):
        if val:
            self.ops[eng].append(lambda E, s=sem, v=val: E.wait_ge(self.sems[s], v))

    def op(self, eng, fn, inc=None):
        if inc is None:
            self.ops[eng].append(fn)
            return None
        sem, amt = inc
        tok = self.bump(sem, amt)
        self.ops[eng].append(
            lambda E, f=fn, s=sem, a=amt: f(E).then_inc(self.sems[s], a))
        return tok

    def dma(self, eng, out, in_, sem):
        return self.op(eng, lambda E, o=out, i=in_: E.dma_start(out=o, in_=i),
                       inc=(sem, 16))

    _DS = {"sync": ("ldw", "syn"), "gpsimd": ("gpd", "gpn")}

    def dsig(self, eng):
        dma_sem, tok_sem = self._DS[eng]
        self.wait(eng, dma_sem, self.cnt.get(dma_sem, 0))
        tok = self.bump(tok_sem, 1)
        self.ops[eng].append(
            lambda E, s=tok_sem: E.sem_inc(self.sems[s], 1))
        return tok

    def barrier(self):
        self.dsig("sync")
        self.dsig("gpsimd")
        s = dict(self.cnt)
        def w(eng, sem):
            self.wait(eng, sem, s.get(sem, 0))
        for eng in ENGINES:
            for sem in ("syn", "gpn", "ten", "vec", "sca"):
                w(eng, sem)


def build_program(cfg, debug=False):
    D, H, T, TT = cfg["D"], cfg["H"], cfg["T"], cfg["TT"]
    KC, HC = D // 128, H // 128
    ECC = KC
    NT = T // TT
    EPS = 1e-5

    nc = bass.Bass("TRN2", target_bir_lowering=False)

    x_h = nc.dram_tensor("x", [128, KC * T], F32, kind="ExternalInput").ap()
    wi_h = nc.dram_tensor("wi", [128, ECC * KC * 128], F32, kind="ExternalInput").ap()
    wf_h = nc.dram_tensor("wf", [128, ECC * KC * 128], F32, kind="ExternalInput").ap()
    wg_h = nc.dram_tensor("wg", [128, ECC * KC * 128], F32, kind="ExternalInput").ap()
    wo_h = nc.dram_tensor("wo", [128, KC * ECC * 128], F32, kind="ExternalInput").ap()
    wqg_h = nc.dram_tensor("wqg", [128, HC * KC * 128], I32, kind="ExternalInput").ap()
    wqu_h = nc.dram_tensor("wqu", [128, HC * KC * 128], I32, kind="ExternalInput").ap()
    wqd_h = nc.dram_tensor("wqd", [128, KC * HC * 128], I32, kind="ExternalInput").ap()
    gam_h = nc.dram_tensor("gamma", [128, ECC], F32, kind="ExternalInput").ap()
    sg_h = nc.dram_tensor("sg", [128, HC], F32, kind="ExternalInput").ap()
    su_h = nc.dram_tensor("su", [128, HC], F32, kind="ExternalInput").ap()
    sd_h = nc.dram_tensor("sd", [128, KC], F32, kind="ExternalInput").ap()
    rmask_h = nc.dram_tensor("rmask", [128, NCORES], F32, kind="ExternalInput").ap()
    rmm_h = nc.dram_tensor("rmm", [128, (NCORES - 1) * KC], F32,
                           kind="ExternalInput").ap()
    rmc_h = nc.dram_tensor("rmc", [128, (NCORES - 1) * KC], F32,
                           kind="ExternalInput").ap()
    out_h = nc.dram_tensor("out", [128, KC * T], F32, kind="ExternalOutput").ap()

    spk = "ExternalOutput" if debug else "Internal"
    d_o = nc.dram_tensor("d_o", [NT, 128, KC * TT], F32, kind=spk).ap()
    d_p = nc.dram_tensor("d_p", [NT, 128, KC * TT], F32, kind=spk).ap()
    d_yh = nc.dram_tensor("d_yh", [NT, 128, KC * TT], BF16, kind=spk).ap()
    d_yl = nc.dram_tensor("d_yl", [NT, 128, KC * TT], BF16, kind=spk).ap()
    d_x2 = nc.dram_tensor("d_x2", [NT, 128, KC * TT], F32, kind=spk).ap()
    d_cr = (nc.dram_tensor("d_cr", [128, KC], F32, kind="ExternalOutput").ap()
            if debug else None)
    d_gate = (nc.dram_tensor("d_gate", [NT, 128, KC * TT], F32,
                             kind="ExternalOutput").ap() if debug else None)
    d_iin = (nc.dram_tensor("d_iin", [NT, 128, KC * TT], F32,
                            kind="ExternalOutput").ap() if debug else None)
    d_init = (nc.dram_tensor("d_init", [NT, 128, 4 * KC], F32,
                             kind="ExternalOutput").ap() if debug else None)
    d_as = (nc.dram_tensor("d_as", [128, NCORES * 2 * KC], F32,
                           kind="ExternalOutput").ap() if debug else None)
    d_rm = (nc.dram_tensor("d_rm", [128, NCORES], F32,
                           kind="ExternalOutput").ap() if debug else None)
    d_vs = (nc.dram_tensor("d_vs", [128, NCORES * 2 * KC], F32,
                           kind="ExternalOutput").ap() if debug else None)
    d_h1 = (nc.dram_tensor("d_h1", [NT, 128, KC * TT], F32,
                           kind="ExternalOutput").ap() if debug else None)
    d_h2 = (nc.dram_tensor("d_h2", [NT, 128, KC * TT], F32,
                           kind="ExternalOutput").ap() if debug else None)
    cc_in = nc.dram_tensor("cc_in", [128, 2 * ECC], F32, kind="Internal").ap()
    cc_out = nc.dram_tensor("cc_out", [NCORES, 128, 2 * ECC], F32,
                            kind="Internal").ap()

    p = Prog()
    es = ExitStack()
    with es:
        SZ = KC * TT
        PAIR = KC * TT // 2
        W5 = 2 * KC * 128
        MISC = 11 * TT
        R7 = max(4 * (KC * 128 // 2), max(0, HC * TT - 2 * SZ))
        offs, o_ = {}, 0
        for name, sz in (("R0", SZ), ("R1", SZ), ("R2", SZ), ("R3", PAIR),
                         ("R4", PAIR), ("R5", W5), ("R6", MISC), ("R7", R7)):
            offs[name] = (o_, sz)
            o_ += sz
        mega = es.enter_context(nc.sbuf_tensor("mega", [128, o_], F32))

        def regf(name, off, n):
            a, sz = offs[name]
            assert off + n <= sz, (name, off, n, sz)
            return mega[:, a + off:a + off + n]

        def reg(name):
            return regf(name, 0, offs[name][1])

        def regb(name, off, n):
            a, sz = offs[name]
            assert off + n <= sz
            return mega[:, a + off:a + off + n].bitcast(BF16)

        def rch(name, c):
            return regf(name, c * TT, TT)

        def pch(name, c):
            return regb(name, c * TT // 2, TT // 2)

        hchunks = []
        for rn in ("R0", "R1", "R7"):
            a, sz = offs[rn]
            for off in range(0, sz - TT // 2 + 1, TT // 2):
                hchunks.append((rn, off))
        assert len(hchunks) >= 2 * HC, (len(hchunks), HC)

        def hhv(i):
            rn, off = hchunks[i]
            return regb(rn, off, TT // 2)

        def hlv(i):
            rn, off = hchunks[HC + i]
            return regb(rn, off, TT // 2)

        def msc(off, n):
            return regf("R6", off, n)

        scr_t = [msc(0, TT), msc(TT, TT)]
        scr_u = [msc(2 * TT, TT), msc(3 * TT, TT)]
        bm_t = msc(4 * TT, TT)
        br_t = msc(5 * TT, TT)
        bc_t = msc(6 * TT, TT)
        scr_v = [msc(7 * TT, TT), msc(8 * TT, TT)]
        scr_w = [msc(9 * TT, TT), msc(10 * TT, TT)]

        WBF = KC * 128 // 2

        def whiv(b):
            return regb("R7", (2 * b) * WBF, WBF)

        def wlov(b):
            return regb("R7", (2 * b + 1) * WBF, WBF)

        def wstv(b):
            return regf("R5", b * KC * 128, KC * 128)

        def ffnv(b):
            return regb("R5", b * WBF, WBF)

        DBF = HC * 128 // 2
        assert DBF <= PAIR

        def wdnv(b):
            return regb("R3" if b == 0 else "R4", 0, DBF)

        zeros = es.enter_context(nc.sbuf_tensor("zeros", [128, TT], F32))
        ones_c = es.enter_context(nc.sbuf_tensor("ones_c", [128, 1], F32))
        ones_r = es.enter_context(nc.sbuf_tensor("ones_r", [1, 128], F32))
        init_o = es.enter_context(nc.sbuf_tensor("init_o", [128, ECC], F32))
        init_p = es.enter_context(nc.sbuf_tensor("init_p", [128, ECC], F32))
        summ = es.enter_context(nc.sbuf_tensor("summ", [128, 2 * ECC], F32))
        asum = es.enter_context(nc.sbuf_tensor("asum", [128, NCORES * 2 * ECC], F32))
        carry = es.enter_context(nc.sbuf_tensor("carry", [128, ECC], F32))
        ctmp = es.enter_context(nc.sbuf_tensor("ctmp", [128, 3 * ECC], F32))
        gam_s = es.enter_context(nc.sbuf_tensor("gam_s", [128, ECC], F32))
        sg_s = es.enter_context(nc.sbuf_tensor("sg_s", [128, HC], F32))
        su_s = es.enter_context(nc.sbuf_tensor("su_s", [128, HC], F32))
        sd_s = es.enter_context(nc.sbuf_tensor("sd_s", [128, KC], F32))
        rm_s = es.enter_context(nc.sbuf_tensor("rm_s", [128, NCORES], F32))
        rmm_s = es.enter_context(nc.sbuf_tensor("rmm_s", [128, (NCORES - 1) * ECC], F32))
        rmc_s = es.enter_context(nc.sbuf_tensor("rmc_s", [128, (NCORES - 1) * ECC], F32))
        eps_s = es.enter_context(nc.sbuf_tensor("eps_s", [1, 1], F32))
        stv = es.enter_context(nc.sbuf_tensor("stv", [1, 6 * TT], F32))
        vsnap = (es.enter_context(nc.sbuf_tensor("vsnap",
                 [128, NCORES * 2 * ECC], F32)) if debug else None)

        def sv(i):
            return stv[0:1, i * TT:(i + 1) * TT]

        ps_full = [es.enter_context(nc.psum_tensor(f"ps{i}", [128, 512], F32))
                   for i in range(8)]
        ps = [t[:, 0:TT] for t in ps_full]

        sems = {}
        for s in ("ldw", "gpd", "ten", "vec", "sca", "cc", "syn", "gpn",
                  "vdm"):
            sems[s] = es.enter_context(nc.semaphore(s))
        p.sems = sems

        p.dma("sync", gam_s[:, :], gam_h, "ldw")
        p.dma("sync", sg_s[:, :], sg_h, "ldw")
        p.dma("sync", su_s[:, :], su_h, "ldw")
        p.dma("sync", sd_s[:, :], sd_h, "ldw")
        p.dma("sync", rm_s[:, :], rmask_h, "ldw")
        p.dma("sync", rmm_s[:, :], rmm_h, "ldw")
        p.dma("sync", rmc_s[:, :], rmc_h, "ldw")
        s_statics = p.dsig("sync")
        p.op("vector", lambda E: E.memset(zeros[:, :], 0.0))
        p.op("vector", lambda E: E.memset(ones_c[:, :], 1.0))
        p.op("vector", lambda E: E.memset(ones_r[:, :], 1.0))
        p.op("vector", lambda E: E.memset(init_o[:, :], 0.0))
        p.op("vector", lambda E: E.memset(eps_s[:, :], EPS))
        t_init = p.op("vector", lambda E: E.memset(init_p[:, :], 1.0),
                      inc=("vec", 1))

        def load_x(st, region, after=()):
            toks = {}
            for s_, v_ in after:
                p.wait("sync", s_, v_)
            for q in range(0, KC, 4):
                for kc in range(q, min(q + 4, KC)):
                    p.dma("sync", rch(region, kc),
                          x_h[:, kc * T + st * TT: kc * T + st * TT + TT],
                          "ldw")
                toks[q // 4] = p.dsig("sync")
            return toks

        def ln_stats(region, t_ready, rdy_sem):
            toks = t_ready if isinstance(t_ready, dict) else None

            def rdy(eng, kc):
                if toks is not None:
                    p.wait(eng, rdy_sem, toks[kc // 4])
                elif t_ready is not None:
                    p.wait(eng, rdy_sem, t_ready)
            p.wait("tensor", "vec", t_init)
            t_sum = None
            for kc in range(KC):
                rdy("tensor", kc)
                t_sum = p.op("tensor", lambda E, k=kc: E.matmul(
                    ps[5][0:1, :], ones_c[:, :], rch(region, k),
                    start=(k == 0), stop=(k == KC - 1)),
                    inc=("ten", 1) if kc == KC - 1 else None)
            sq_mm = {}
            t_sq = None
            for kc in range(KC):
                if kc >= 2:
                    p.wait("scalar", "ten", sq_mm[kc - 2])
                rdy("scalar", kc)
                tk = p.op("scalar", lambda E, k=kc: E.activation(
                    out=scr_u[k % 2][:, :], in_=rch(region, k), func=AF.Square),
                    inc=("sca", 1))
                p.wait("tensor", "sca", tk)
                t_sq = sq_mm[kc] = p.op("tensor", lambda E, k=kc: E.matmul(
                    ps[4][0:1, :], ones_c[:, :], scr_u[k % 2][:, :],
                    start=(k == 0), stop=(k == KC - 1)), inc=("ten", 1))
            p.wait("vector", "ten", max(t_sum, t_sq))
            wl0 = p.op("vector", lambda E: E.tensor_scalar(
                out=sv(0), in0=ps[5][0:1, :], scalar1=1.0 / D, scalar2=None,
                op0=OP.mult), inc=("vec", 1))
            p.wait("vector", "vec", wl0)
            wls = p.op("vector", lambda E: E.tensor_scalar(
                out=sv(1), in0=ps[4][0:1, :], scalar1=1.0 / D, scalar2=None,
                op0=OP.mult), inc=("vec", 1))
            p.wait("vector", "vec", wls)
            p.op("vector", lambda E: E.tensor_mul(sv(2), sv(0), sv(0)))
            t_var = p.op("vector", lambda E: E.tensor_sub(sv(3), sv(1), sv(2)),
                         inc=("vec", 1))
            t_r = _rsqrt_vec(sv(3), t_var, 1, 2)
            p.wait("tensor", "vec", t_r)
            p.op("tensor", lambda E: E.matmul(ps[6][:, :], ones_r[:, :], sv(0),
                                              start=True, stop=True))
            t_bc = p.op("tensor", lambda E: E.matmul(
                ps[7][:, :], ones_r[:, :], sv(5), start=True, stop=True),
                inc=("ten", 1))
            p.wait("vector", "ten", t_bc)
            p.op("vector", lambda E: E.tensor_copy(out=bm_t, in_=ps[6][:, :]))
            return p.op("vector", lambda E: E.tensor_copy(out=br_t, in_=ps[7][:, :]),
                        inc=("vec", 1))

        def _rsqrt_vec(v, t_v, ia, ib):
            a, b = sv(ia), sv(ib)
            w0 = p.op("vector", lambda E: E.tensor_scalar(
                out=sv(4), in0=v, scalar1=EPS, scalar2=None, op0=OP.add),
                inc=("vec", 1))
            p.wait("scalar", "vec", p.cnt["vec"])
            t_sqr = p.op("scalar", lambda E: E.activation(
                out=sv(5), in_=v, func=AF.Sqrt, bias=eps_s[0:1, 0:1]),
                inc=("sca", 1))
            p.wait("vector", "sca", t_sqr)
            p.wait("vector", "vec", w0)
            p.op("vector", lambda E: E.reciprocal(a, sv(5)))
            p.op("vector", lambda E: E.tensor_mul(b, a, a))
            p.op("vector", lambda E: E.tensor_mul(sv(5), b, sv(4)))
            w1 = p.op("vector", lambda E: E.tensor_scalar(
                out=b, in0=sv(5), scalar1=-0.5, scalar2=1.5,
                op0=OP.mult, op1=OP.add), inc=("vec", 1))
            p.wait("vector", "vec", w1)
            return p.op("vector", lambda E: E.tensor_mul(sv(5), a, b),
                        inc=("vec", 1))

        import os as _os
        _noln = bool(_os.environ.get("KDBG_NOLN"))

        def ln_apply_split(region, t_stats):
            p.wait("vector", "vec", t_stats)
            tk = None
            for kc in range(KC):
                s = scr_t[kc % 2]
                if not _noln:
                    p.op("vector", lambda E, k=kc, s=s: E.tensor_sub(
                        s, rch(region, k), bm_t))
                    p.op("vector", lambda E, k=kc, s=s: E.tensor_mul(
                        rch(region, k), s, br_t))
                p.op("vector", lambda E, k=kc: E.tensor_copy(
                    out=pch("R3", k), in_=rch(region, k)))
                tk = p.op("vector", lambda E, k=kc: E.tensor_tensor(
                    out=pch("R4", k), in0=rch(region, k), in1=pch("R3", k),
                    op=OP.subtract), inc=("vec", 1))
            return tk

        def projection(w_h, n_out, nk, rhs_hi, rhs_lo, evac, rhs_waits,
                       first_waits=None):
            snap0 = dict(p.cnt)
            state = {}

            def stage(oc):
                b = oc % 2
                wst, whi, wlo = wstv(b), whiv(b), wlov(b)
                prev = state.get(("split", b))
                if prev is not None:
                    p.wait("sync", "vec", prev)
                elif first_waits is not None:
                    for s_, v_ in first_waits["stage"]:
                        p.wait("sync", s_, v_)
                else:
                    p.wait("sync", "vec", snap0.get("vec", 0))
                    p.wait("sync", "sca", snap0.get("sca", 0))
                p.dma("sync", wst,
                      w_h[:, oc * nk * 128:(oc + 1) * nk * 128], "ldw")
                s_w = p.dsig("sync")
                tb = state.get(("used", b))
                if tb is None:
                    tb = (first_waits or {}).get("ten", snap0.get("ten", 0))
                p.wait("vector", "syn", s_w)
                p.wait("vector", "ten", tb)
                p.op("vector", lambda E, w=wst, h=whi: E.tensor_copy(
                    out=h, in_=w))
                t_lo = p.op("vector", lambda E, w=wst, h=whi, l=wlo:
                            E.tensor_tensor(out=l, in0=w, in1=h,
                                            op=OP.subtract), inc=("vec", 1))
                state[("split", b)] = t_lo
                state[("ready", oc)] = t_lo

            stage(0)
            if n_out > 1:
                stage(1)
            for oc in range(n_out):
                b = oc % 2
                whi, wlo = whiv(b), wlov(b)
                p.wait("tensor", "vec", state[("ready", oc)])
                if oc == 0:
                    for s_, v_ in rhs_waits:
                        p.wait("tensor", s_, v_)
                bf = state.get(("bank", b))
                if bf is not None:
                    p.wait("tensor", bf[0], bf[1])
                elif first_waits is not None:
                    for s_, v_ in first_waits["bank"]:
                        p.wait("tensor", s_, v_)
                else:
                    p.wait("tensor", "sca", snap0.get("sca", 0))
                    p.wait("tensor", "vec", snap0.get("vec", 0))
                t_grp = None
                pb = ps[b]
                for kc in range(nk):
                    wh = whi[:, kc * 128:(kc + 1) * 128]
                    wl = wlo[:, kc * 128:(kc + 1) * 128]
                    p.op("tensor", lambda E, a=pb, w=wh, r=rhs_hi(kc), st=(kc == 0):
                         E.matmul(a[:, :], w, r, start=st, stop=False))
                    p.op("tensor", lambda E, a=pb, w=wh, r=rhs_lo(kc):
                         E.matmul(a[:, :], w, r, start=False, stop=False))
                    t_grp = p.op(
                        "tensor",
                        lambda E, a=pb, w=wl, r=rhs_hi(kc), sp=(kc == nk - 1):
                        E.matmul(a[:, :], w, r, start=False, stop=sp),
                        inc=("ten", 1) if kc == nk - 1 else None)
                state[("used", b)] = t_grp
                state[("bank", b)] = evac(oc, ps[b][:, :], t_grp)
                if oc + 2 < n_out:
                    stage(oc + 2)
            return state

        spill_tok = {}
        prev_i = None
        for st in range(NT):
            if st == 0:
                p.barrier()
            t_x = load_x(st, "R0",
                         after=([("gpn", spill_tok[("o", st - 1)])]
                                if st > 0 else ()))
            if _os.environ.get("KDBG_NOSTATS"):
                p.wait("vector", "syn", max(t_x.values()))
                t_st = p.op("vector", lambda E: E.engine_nop(), inc=("vec", 1))
            else:
                t_st = ln_stats("R0", t_x, "syn")
            if st > 0:
                p.wait("vector", "gpn", spill_tok[("rest", st - 1)])
                p.wait("scalar", "gpn", spill_tok[("rest", st - 1)])
                p.wait("vector", "vdm", spill_tok[("y", st - 1)])
                p.wait("scalar", "vdm", spill_tok[("y", st - 1)])
            t_pair = ln_apply_split("R0", t_st)

            evF = {}

            def evac_f(oc, psap, t_grp):
                p.wait("scalar", "ten", t_grp)
                import os as _os
                if _os.environ.get("KDBG_NOSIG"):
                    tk = p.op("scalar", lambda E, o=oc, a=psap: E.copy(
                        out=rch("R1", o), in_=a), inc=("sca", 1))
                else:
                    tk = p.op("scalar", lambda E, o=oc, a=psap: E.activation(
                        out=rch("R1", o), in_=a, func=AF.Sigmoid), inc=("sca", 1))
                evF[oc] = tk
                return ("sca", tk)

            if prev_i is None:
                f_fw = {"stage": [], "bank": [], "ten": 0}
            else:
                f_fw = {"stage": [("vec", prev_i[("ready", ECC - 1)])],
                        "bank": [prev_i[("bank", 0)], prev_i[("bank", 1)]],
                        "ten": max(prev_i[("used", 0)], prev_i[("used", 1)])}
            f_state = projection(wf_h, ECC, KC, lambda kc: pch("R3", kc),
                                 lambda kc: pch("R4", kc), evac_f,
                                 rhs_waits=[("vec", t_pair)],
                                 first_waits=f_fw)

            evI = {}
            t_scan_tok = {}

            def evac_i(oc, psap, t_grp):
                p.wait("scalar", "ten", t_grp)
                if oc >= 2:
                    p.wait("scalar", "vec", evI[oc - 2])
                t_si = p.op("scalar", lambda E, o=oc, a=psap: E.activation(
                    out=scr_u[o % 2][:, :], in_=a, func=AF.Silu), inc=("sca", 1))
                p.wait("vector", "sca", t_si)
                tk = p.op("vector", lambda E, o=oc: E.scalar_tensor_tensor(
                    out=rch("R2", o), in0=rch("R1", o), scalar=1.0,
                    in1=scr_u[o % 2][:, :], op0=OP.subtract, op1=OP.mult),
                    inc=("vec", 1))
                evI[oc] = tk
                p.wait("vector", "sca", evF[oc])
                p.wait("vector", "vec", tk)
                tso = p.op("vector", lambda E, e=oc: E.tensor_tensor_scan(
                    out=rch("R0", e), data0=rch("R1", e), data1=rch("R2", e),
                    initial=init_o[:, e:e + 1], op0=OP.mult, op1=OP.subtract),
                    inc=("vec", 1))
                p.wait("vector", "vec", tso)
                p.op("vector", lambda E, e=oc: E.tensor_copy(
                    out=init_o[:, e:e + 1], in_=rch("R0", e)[:, TT - 1:TT]))
                tsp = p.op("vector", lambda E, e=oc: E.tensor_tensor_scan(
                    out=rch("R2", e), data0=rch("R1", e), data1=zeros[:, :],
                    initial=init_p[:, e:e + 1], op0=OP.mult, op1=OP.add),
                    inc=("vec", 1))
                p.wait("vector", "vec", tsp)
                t_scan_tok[oc] = p.op("vector", lambda E, e=oc: E.tensor_copy(
                    out=init_p[:, e:e + 1], in_=rch("R2", e)[:, TT - 1:TT]),
                    inc=("vec", 1))
                return ("sca", t_si)

            prev_i = projection(
                wi_h, ECC, KC, lambda kc: pch("R3", kc),
                lambda kc: pch("R4", kc), evac_i,
                rhs_waits=[("vec", t_pair)],
                first_waits={
                    "stage": [("vec", f_state[("ready", ECC - 1)])],
                    "bank": [f_state[("bank", 0)], f_state[("bank", 1)]],
                    "ten": max(f_state[("used", 0)], f_state[("used", 1)])})
            t_scan = t_scan_tok[ECC - 1]

            if st < NT - 1:
                for q in range(0, ECC, 4):
                    hi = min(q + 4, ECC) - 1
                    p.wait("gpsimd", "vec", t_scan_tok[hi])
                    p.dma("gpsimd", d_o[st][:, q * TT:(hi + 1) * TT],
                          regf("R0", q * TT, (hi + 1 - q) * TT), "gpd")
                spill_tok[("o", st)] = p.dsig("gpsimd")
                p.dma("gpsimd", d_p[st], reg("R2"), "gpd")
                p.dma("scalar", d_yh[st], reg("R3").bitcast(BF16), "vdm")
                p.dma("scalar", d_yl[st], reg("R4").bitcast(BF16), "vdm")
                spill_tok[("rest", st)] = p.dsig("gpsimd")
                spill_tok[("y", st)] = p.cnt["vdm"]

        p.wait("vector", "vec", t_scan)
        p.op("vector", lambda E: E.tensor_copy(out=summ[:, 0:ECC],
                                               in_=init_p[:, :]))
        t_summ = p.op("vector", lambda E: E.tensor_copy(
            out=summ[:, ECC:2 * ECC], in_=init_o[:, :]), inc=("vec", 1))
        p.barrier()
        snapP2 = dict(p.cnt)
        p.wait("gpsimd", "vec", t_summ)
        p.dma("gpsimd", cc_in, summ[:, :], "gpd")
        p.wait("gpsimd", "gpd", p.cnt["gpd"])
        t_cc = p.op("gpsimd", lambda E: E.collective_compute(
            "AllGather", OP.bypass, replica_groups=[list(range(NCORES))],
            ins=[cc_in], outs=[cc_out]), inc=("cc", 1))
        p.wait("gpsimd", "cc", t_cc)
        for r in range(NCORES):
            p.dma("gpsimd", asum[:, r * 2 * ECC:(r + 1) * 2 * ECC],
                  cc_out[r], "gpd")
        g_as = p.dsig("gpsimd")
        if debug:
            p.dma("gpsimd", d_as, asum[:, :], "gpd")
            p.dma("gpsimd", d_rm, rm_s[:, :], "gpd")

        def gop(fn):
            tok = p.op("gpsimd", fn, inc=("gpn", 1))
            p.wait("gpsimd", "gpn", tok)
            return tok
        gop(lambda E: E.memset(carry[:, :], 0.0))
        for r in range(NCORES - 1):
            A_r = asum[:, r * 2 * ECC: r * 2 * ECC + ECC]
            B_r = asum[:, r * 2 * ECC + ECC: (r + 1) * 2 * ECC]
            mm_r = rmm_s[:, r * ECC:(r + 1) * ECC]
            mc_r = rmc_s[:, r * ECC:(r + 1) * ECC]
            tA, tB, tC = (ctmp[:, 0:ECC], ctmp[:, ECC:2 * ECC],
                          ctmp[:, 2 * ECC:3 * ECC])
            gop(lambda E, A=A_r, m=mm_r, t=tC: E.tensor_mul(t, A, m))
            gop(lambda E, t=tA, c=tC, m=mc_r: E.tensor_add(t, c, m))
            gop(lambda E, B=B_r, m=mm_r, t=tB: E.tensor_mul(t, B, m))
            gop(lambda E, t=tC, a=tA: E.tensor_mul(t, carry[:, :], a))
            gop(lambda E, t=tC, b=tB: E.tensor_add(carry[:, :], t, b))
        t_carry = gop(lambda E: E.tensor_copy(out=ctmp[:, 0:ECC],
                                              in_=carry[:, :]))
        if debug:
            p.dma("gpsimd", d_cr, carry[:, :], "gpd")

        def prh(rn, ec):
            return regf(rn, ec * TT, TT // 2).bitcast(BF16)

        def prl(rn, ec):
            return regf(rn, ec * TT + TT // 2, TT // 2).bitcast(BF16)

        assert NT == 2
        P2MAPS = {1: dict(o="R0", pp="R2", g="R1"),
                  0: dict(o="R2", pp="R1", g="R0")}

        def p2_body(st, first, rd_toks, extra_gwait, prefetch,
                    g_first_waits=None):
            Ro, Rp, Rg = (P2MAPS[st][k] for k in ("o", "pp", "g"))
            t_rdp, t_rd = rd_toks if rd_toks else (None, None)
            if first:
                for eng in ENGINES:
                    for sem in ("syn", "gpn", "ten", "vec", "sca"):
                        p.wait(eng, sem, snapP2.get(sem, 0))
            else:
                p.barrier()

            g2mm = {}
            g2sq = {}
            t_sil = {}
            t_oa_tok = {}
            DLY = min(3, ECC)

            def emit_g2(o):
                p.wait("tensor", "sca", g2sq[o])
                g2mm[o] = p.op("tensor", lambda E, o=o: E.matmul(
                    ps[4][0:1, :], ones_c[:, :], scr_u[o % 2][:, :],
                    start=(o == 0), stop=(o == ECC - 1)), inc=("ten", 1))

            def chain(ec):
                u = scr_w[ec % 2]
                if ec == 0:
                    p.wait("vector", "gpn", t_carry)
                    if t_rd is not None:
                        p.wait("vector", "gpn", t_rd)
                if ec >= 2:
                    p.wait("vector", "sca", t_sil[ec - 2])
                tf = p.op("vector", lambda E, e=ec, u=u: E.scalar_tensor_tensor(
                    out=u, in0=rch(Rp, e), scalar=carry[:, e:e + 1],
                    in1=rch(Ro, e), op0=OP.mult, op1=OP.add), inc=("vec", 1))
                p.wait("scalar", "vec", tf)
                t_sil[ec] = p.op("scalar", lambda E, e=ec, u=u: E.activation(
                    out=rch(Rp, e), in_=u, func=AF.Silu), inc=("sca", 1))
                p.wait("vector", "sca", t_sil[ec])
                p.op("vector", lambda E, e=ec: E.tensor_mul(
                    rch(Ro, e), rch(Rg, e), rch(Rp, e)))
                whi_ = p.op("vector", lambda E, e=ec: E.tensor_scalar(
                    out=prh(Rp, e), in0=rch(Ro, e),
                    scalar1=gam_s[:, e:e + 1], scalar2=None, op0=OP.mult),
                    inc=("vec", 1))
                p.wait("vector", "vec", whi_)
                t_oa_tok[ec] = p.op("vector", lambda E, e=ec: E.scalar_tensor_tensor(
                    out=prl(Rp, e), in0=rch(Ro, e),
                    scalar=gam_s[:, e:e + 1], in1=prh(Rp, e),
                    op0=OP.mult, op1=OP.subtract), inc=("vec", 1))

            def evac_g(oc, psap, t_grp):
                p.wait("vector", "ten", t_grp)
                if oc == 0 and extra_gwait is not None:
                    p.wait("vector", "gpn", extra_gwait)
                t_cp = p.op("vector", lambda E, o=oc, a=psap: E.tensor_copy(
                    out=rch(Rg, o), in_=a), inc=("vec", 1))
                if oc >= 2:
                    p.wait("scalar", "ten", g2mm[oc - 2])
                p.wait("scalar", "vec", t_cp)
                g2sq[oc] = p.op("scalar", lambda E, o=oc: E.activation(
                    out=scr_u[o % 2][:, :], in_=rch(Rg, o), func=AF.Square),
                    inc=("sca", 1))
                if oc >= 1:
                    emit_g2(oc - 1)
                if oc >= DLY:
                    chain(oc - DLY)
                return ("vec", t_cp)

            gw = [("gpn", t_rdp)] if t_rdp is not None else []
            projection(wg_h, ECC, KC, lambda kc: pch("R3", kc),
                       lambda kc: pch("R4", kc), evac_g, rhs_waits=gw,
                       first_waits=g_first_waits)
            t_glast = p.cnt["ten"]
            if prefetch is not None:
                prefetch("after_g", ten=t_glast)
            emit_g2(ECC - 1)
            p.wait("vector", "ten", g2mm[ECC - 1])
            t_ms = p.op("vector", lambda E: E.tensor_scalar(
                out=sv(1), in0=ps[4][0:1, :], scalar1=1.0 / D, scalar2=None,
                op0=OP.mult), inc=("vec", 1))
            t_r2 = _rsqrt_vec(sv(1), t_ms, 2, 3)
            p.wait("tensor", "vec", t_r2)
            t_bc2 = p.op("tensor", lambda E: E.matmul(
                ps[6][:, :], ones_r[:, :], sv(5), start=True, stop=True),
                inc=("ten", 1))
            p.wait("vector", "ten", t_bc2)
            t_brd = p.op("vector", lambda E: E.tensor_copy(
                out=br_t, in_=ps[6][:, :]), inc=("vec", 1))

            for ec in range(max(0, ECC - DLY), ECC):
                chain(ec)
            t_oa = t_oa_tok[ECC - 1]

            p.wait("gpsimd", "vec", t_oa)
            for kc in range(KC):
                p.dma("gpsimd", rch(Rg, kc),
                      x_h[:, kc * T + st * TT: kc * T + st * TT + TT], "gpd")
            t_xr = p.dsig("gpsimd")

            ew_tok = {}

            def evac_wo(oc, psap, t_grp):
                p.wait("vector", "ten", t_grp)
                if oc == 0:
                    p.wait("vector", "gpn", t_xr)
                    p.wait("vector", "vec", t_brd)
                s = scr_t[oc % 2]
                p.op("vector", lambda E, o=oc, a=psap, s=s: E.tensor_mul(
                    s, a, br_t))
                tk = p.op("vector", lambda E, o=oc, s=s: E.tensor_add(
                    rch(Ro, o), s, rch(Rg, o)), inc=("vec", 1))
                ew_tok[oc] = tk
                return ("vec", tk)

            wo_state = projection(wo_h, KC, ECC, lambda ec: prh(Rp, ec),
                                  lambda ec: prl(Rp, ec), evac_wo,
                                  rhs_waits=[("vec", t_oa)])
            t_wolast_ten = p.cnt["ten"]
            t_wolast_vec = p.cnt["vec"]
            if prefetch is not None:
                prefetch("after_wo", ten=t_wolast_ten, vec=t_wolast_vec)
            return t_wolast_vec, ew_tok, wo_state

        pf_toks = {}

        def prefetch_st0(stage_name, ten=None, vec=None):
            if stage_name == "after_g":
                p.wait("gpsimd", "ten", ten)
                p.wait("gpsimd", "vdm", spill_tok[("y", 0)])
                p.dma("gpsimd", reg("R3").bitcast(BF16), d_yh[0], "gpd")
                p.dma("gpsimd", reg("R4").bitcast(BF16), d_yl[0], "gpd")
                pf_toks["rdp"] = p.dsig("gpsimd")
            else:
                p.wait("gpsimd", "ten", ten)
                p.dma("gpsimd", reg("R2"), d_o[0], "gpd")
                p.wait("gpsimd", "vec", vec)
                p.dma("gpsimd", reg("R1"), d_p[0], "gpd")
                pf_toks["rd"] = p.dsig("gpsimd")

        t_x2v, ew1, wo1 = p2_body(NT - 1, True, None, None, prefetch_st0)
        p.wait("gpsimd", "vec", t_x2v)
        p.dma("gpsimd", d_x2[1], reg("R0"), "gpd")
        t_x2sp = p.dsig("gpsimd")

        tight = {"stage": [("vec", wo1[("ready", KC - 1)])],
                 "bank": [("vec", ew1[KC - 1])]}
        _, ew0, _ = p2_body(0, False, (pf_toks["rdp"], pf_toks["rd"]),
                            t_x2sp, None, g_first_waits=tight)

        prev_ev = [None]
        out_tok_prev = [None]
        for st in range(NT):
            if st > 0:
                ev0 = prev_ev[0]
                snapP3b = dict(p.cnt)
                for eng in ("gpsimd", "scalar", "tensor", "vector"):
                    for sem in ("ten", "vec", "sca"):
                        p.wait(eng, sem, snapP3b.get(sem, 0))
                p.wait("sync", "gpn", snapP3b.get("gpn", 0))
                rd_toks3 = {}
                for q in range(0, KC, 4):
                    hi = min(q + 4, KC) - 1
                    p.wait("sync", "vec", ev0[hi])
                    p.wait("sync", "syn", out_tok_prev[0][hi])
                    p.dma("sync", d_x2[st][:, q * TT:(hi + 1) * TT] if False
                          else reg("R2")[:, q * TT:(hi + 1) * TT],
                          d_x2[st][:, q * TT:(hi + 1) * TT], "ldw")
                    rd_toks3[q // 4] = p.dsig("sync")
                t_st3 = ln_stats("R2", rd_toks3, "syn")
            else:
                snapP3 = dict(p.cnt)
                p.wait("gpsimd", "sca", snapP3.get("sca", 0))
                p.wait("gpsimd", "ten", snapP3.get("ten", 0))
                p.wait("gpsimd", "vec", snapP3.get("vec", 0))
                ew_batches = {q: ew0[min(q * 4 + 3, KC - 1)]
                              for q in range((KC + 3) // 4)}
                t_st3 = ln_stats("R2", ew_batches, "vec")
            p.wait("vector", "vec", t_st3)
            t_p3 = None
            for kc in range(KC):
                s, u = scr_t[kc % 2], scr_w[kc % 2]
                p.op("vector", lambda E, k=kc, s=s: E.tensor_sub(
                    s, rch("R2", k), bm_t))
                p.op("vector", lambda E, s=s, u=u: E.tensor_mul(u, s, br_t))
                p.op("vector", lambda E, k=kc, u=u: E.tensor_copy(
                    out=pch("R3", k), in_=u))
                t_p3 = p.op("vector", lambda E, k=kc, u=u: E.tensor_tensor(
                    out=pch("R4", k), in0=u, in1=pch("R3", k), op=OP.subtract),
                    inc=("vec", 1))

            wu_used = {}
            sg_tok = {}
            hf_tok = {}
            cs_tok = {}
            t_h = {}
            gbank = {}
            ubank = {}

            def emit_cs(h):
                p.wait("tensor", "vec", t_h[h])
                cs_tok[h] = p.op("tensor", lambda E, h=h, b=scr_w[h % 2]:
                                 E.matmul(ps[4][0:1, :], ones_c[:, :], b,
                                          start=(h == 0), stop=(h == HC - 1)),
                                 inc=("ten", 1))

            for hc in range(HC):
                bg, bu = hc % 2, 2 + hc % 2
                tb = wu_used.get(bg)
                if tb is not None:
                    p.wait("gpsimd", "ten", tb)
                p.dma("gpsimd", ffnv(bg),
                      wqg_h[:, hc * KC * 128:(hc + 1) * KC * 128], "gpd")
                p.dma("gpsimd", ffnv(2 + bg),
                      wqu_h[:, hc * KC * 128:(hc + 1) * KC * 128], "gpd")
                t_w2 = p.dsig("gpsimd")
                p.wait("tensor", "gpn", t_w2)
                if hc == 0:
                    p.wait("tensor", "vec", t_p3)
                bf = gbank.get(bg)
                if bf is not None:
                    p.wait("tensor", "sca", bf)
                t_gg = None
                for kc in range(KC):
                    w = ffnv(bg)[:, kc * 128:(kc + 1) * 128]
                    p.op("tensor", lambda E, a=ps[bg], w=w, k=kc: E.matmul(
                        a[:, :], w, pch("R3", k), start=(k == 0),
                        stop=False))
                    t_gg = p.op("tensor", lambda E, a=ps[bg], w=w, k=kc: E.matmul(
                        a[:, :], w, pch("R4", k), start=False,
                        stop=(k == KC - 1)),
                        inc=("ten", 1) if kc == KC - 1 else None)
                p.wait("scalar", "ten", t_gg)
                if hc >= 2:
                    p.wait("scalar", "vec", hf_tok[hc - 2])
                t_sg2 = p.op("scalar", lambda E, h=hc, b=bg: E.activation(
                    out=scr_u[h % 2][:, :], in_=ps[b][:, :], func=AF.Silu,
                    scale=sg_s[:, h:h + 1]), inc=("sca", 1))
                sg_tok[hc] = t_sg2
                gbank[bg] = t_sg2
                bf = ubank.get(bu)
                if bf is not None:
                    p.wait("tensor", "vec", bf)
                t_ug = None
                for kc in range(KC):
                    w = ffnv(2 + bg)[:, kc * 128:(kc + 1) * 128]
                    p.op("tensor", lambda E, a=ps[bu], w=w, k=kc: E.matmul(
                        a[:, :], w, pch("R3", k), start=(k == 0),
                        stop=False))
                    t_ug = p.op("tensor", lambda E, a=ps[bu], w=w, k=kc: E.matmul(
                        a[:, :], w, pch("R4", k), start=False,
                        stop=(k == KC - 1)),
                        inc=("ten", 1) if kc == KC - 1 else None)
                wu_used[bg] = t_ug
                p.wait("vector", "ten", t_ug)
                p.wait("vector", "sca", t_sg2)
                if hc >= 2:
                    p.wait("vector", "ten", cs_tok[hc - 2])
                hbuf = scr_w[hc % 2]
                t_hf = p.op("vector",
                            lambda E, h=hc, b=hbuf, u=bu: E.scalar_tensor_tensor(
                                out=b, in0=ps[u][:, :], scalar=su_s[:, h:h + 1],
                                in1=scr_u[h % 2][:, :], op0=OP.mult, op1=OP.mult),
                            inc=("vec", 1))
                hf_tok[hc] = t_hf
                ubank[bu] = t_hf
                p.wait("vector", "vec", t_hf)
                p.op("vector", lambda E, h=hc, b=hbuf: E.tensor_copy(
                    out=hhv(h), in_=b))
                t_h[hc] = p.op("vector", lambda E, h=hc, b=hbuf: E.tensor_tensor(
                    out=hlv(h), in0=b, in1=hhv(h), op=OP.subtract),
                    inc=("vec", 1))
                if hc >= 1:
                    emit_cs(hc - 1)

            emit_cs(HC - 1)
            t_cs = cs_tok[HC - 1]
            p.wait("vector", "ten", t_cs)
            t_cs2 = p.op("vector", lambda E: E.tensor_scalar(
                out=sv(0), in0=ps[4][0:1, :], scalar1=128.0, scalar2=None,
                op0=OP.mult), inc=("vec", 1))
            p.wait("tensor", "vec", t_cs2)
            t_bc3 = p.op("tensor", lambda E: E.matmul(
                ps[6][:, :], ones_r[:, :], sv(0), start=True, stop=True),
                inc=("ten", 1))
            p.wait("vector", "ten", t_bc3)
            t_bcd = p.op("vector", lambda E: E.tensor_copy(
                out=bc_t, in_=ps[6][:, :]), inc=("vec", 1))

            p.wait("tensor", "sca", p.cnt["sca"])
            p.wait("tensor", "vec", p.cnt["vec"])
            dn_used = {}
            dbank = {}
            out_tok = {}
            out_tok_prev[0] = out_tok
            ev_tok = prev_ev[0] = {}
            for dc in range(KC):
                b = dc % 2
                tb = dn_used.get(b)
                if tb is not None:
                    p.wait("gpsimd", "ten", tb)
                if dc < 2:
                    p.wait("gpsimd", "ten", t_cs)
                p.dma("gpsimd", wdnv(b),
                      wqd_h[:, dc * HC * 128:(dc + 1) * HC * 128], "gpd")
                t_wd = p.dsig("gpsimd")
                p.wait("tensor", "gpn", t_wd)
                bf = dbank.get(b)
                if bf is not None:
                    p.wait("tensor", "vec", bf)
                t_dg = None
                for hc in range(HC):
                    w = wdnv(b)[:, hc * 128:(hc + 1) * 128]
                    p.op("tensor", lambda E, a=ps[b], w=w, h=hc: E.matmul(
                        a[:, :], w, hhv(h), start=(h == 0), stop=False))
                    t_dg = p.op("tensor", lambda E, a=ps[b], w=w, h=hc: E.matmul(
                        a[:, :], w, hlv(h), start=False,
                        stop=(h == HC - 1)),
                        inc=("ten", 1) if hc == HC - 1 else None)
                dn_used[b] = t_dg
                p.wait("vector", "ten", t_dg)
                p.wait("vector", "vec", t_bcd)
                if dc >= 2:
                    p.wait("vector", "syn", out_tok[dc - 2])
                ob = scr_v[dc % 2]
                p.op("vector", lambda E, a=ps[b][:, :], o=ob: E.tensor_sub(
                    o, a, bc_t))
                t_ev = p.op("vector", lambda E, o=ob, d=dc: E.scalar_tensor_tensor(
                    out=o, in0=o, scalar=sd_s[:, d:d + 1], in1=rch("R2", d),
                    op0=OP.mult, op1=OP.add), inc=("vec", 1))
                dbank[b] = t_ev
                ev_tok[dc] = t_ev
                p.wait("sync", "vec", t_ev)
                p.dma("sync", out_h[:, dc * T + st * TT: dc * T + st * TT + TT],
                      ob, "ldw")
                out_tok[dc] = p.dsig("sync")

        p.wait("sync", "ldw", p.cnt["ldw"])
        p.wait("gpsimd", "gpd", p.cnt["gpd"])

        block = es.enter_context(nc.Block())

        def emit(name):
            def body(E):
                for fn in p.ops[name]:
                    fn(E)
            return body

        block.sync(emit("sync"))
        block.tensor(emit("tensor"))
        block.vector(emit("vector"))
        block.scalar(emit("scalar"))
        block.gpsimd(emit("gpsimd"))

    return nc



def _img_act(xs, T, KC):
    return np.ascontiguousarray(
        xs.reshape(T, KC, 128).transpose(2, 1, 0).reshape(128, KC * T))


def _img_w(WT, n_in_c, n_out_c):
    r = WT.reshape(n_in_c, 128, n_out_c, 128).transpose(1, 2, 0, 3)
    return np.ascontiguousarray(r.reshape(128, n_out_c * n_in_c * 128))


def _img_vec(v, nchunks):
    return np.ascontiguousarray(v.reshape(nchunks, 128).T)


def run(cfg, x, Wi, Wf, Wg, gam, Wo, wqg, sgv, wqu, suv, wqd, sdv,
        debug=False):
    D, H, T = cfg["D"], cfg["H"], cfg["T"]
    KC, HC = D // 128, H // 128
    wi_img = _img_w(np.ascontiguousarray(Wi.T), KC, KC)
    wf_img = _img_w(np.ascontiguousarray(Wf.T), KC, KC)
    wg_img = _img_w(np.ascontiguousarray(Wg.T), KC, KC)
    wo_img = _img_w(np.ascontiguousarray(Wo.T), KC, KC)
    wqg_img = _img_w(np.ascontiguousarray(wqg.T), KC, HC)
    wqu_img = _img_w(np.ascontiguousarray(wqu.T), KC, HC)
    wqd_img = _img_w(np.ascontiguousarray(wqd.T), HC, KC)
    gam_img = _img_vec(gam, KC)
    sg_img = _img_vec(sgv, HC)
    su_img = _img_vec(suv, HC)
    sd_img = _img_vec(sdv, KC)

    nc = build_program(cfg, debug=debug)
    in_maps = []
    for k in range(NCORES):
        xs = x[k * T:(k + 1) * T]
        rm = np.tile((np.arange(NCORES) < k).astype(np.float32), (128, 1))
        mvec = (np.arange(NCORES - 1) < k).astype(np.float32)
        rmm = np.tile(np.repeat(mvec, KC), (128, 1)).astype(np.float32)
        rmc = np.tile(np.repeat(1.0 - mvec, KC), (128, 1)).astype(np.float32)
        in_maps.append({
            "x": _img_act(xs, T, KC),
            "wi": wi_img, "wf": wf_img, "wg": wg_img, "wo": wo_img,
            "wqg": wqg_img, "wqu": wqu_img, "wqd": wqd_img,
            "gamma": gam_img, "sg": sg_img, "su": su_img, "sd": sd_img,
            "rmask": np.ascontiguousarray(rm),
            "rmm": np.ascontiguousarray(rmm),
            "rmc": np.ascontiguousarray(rmc),
        })
    res = run_bass_kernel_spmd(nc, in_maps, core_ids=list(range(NCORES)))
    outs = []
    for k in range(NCORES):
        o = np.asarray(res.results[k]["out"])
        outs.append(o.reshape(128, KC, T).transpose(2, 1, 0).reshape(T, D))
    full = np.concatenate(outs, axis=0)
    if debug:
        return full, res.results
    return full


def kernel(L0_attn_hidden_states, L0_attn_Wi, L0_attn_Wf, L0_attn_Wg,
           L0_attn_g_norm_weight, L0_attn_Wo, L0_ffn_wq_gate, L0_ffn_s_gate,
           L0_ffn_wq_up, L0_ffn_s_up, L0_ffn_wq_down, L0_ffn_s_down):
    cfg = CFG_REAL
    out = run(
        cfg,
        np.asarray(L0_attn_hidden_states, dtype=np.float32)[0],
        np.asarray(L0_attn_Wi, dtype=np.float32),
        np.asarray(L0_attn_Wf, dtype=np.float32),
        np.asarray(L0_attn_Wg, dtype=np.float32),
        np.asarray(L0_attn_g_norm_weight, dtype=np.float32),
        np.asarray(L0_attn_Wo, dtype=np.float32),
        np.asarray(L0_ffn_wq_gate, dtype=np.int32),
        np.asarray(L0_ffn_s_gate, dtype=np.float32),
        np.asarray(L0_ffn_wq_up, dtype=np.int32),
        np.asarray(L0_ffn_s_up, dtype=np.float32),
        np.asarray(L0_ffn_wq_down, dtype=np.int32),
        np.asarray(L0_ffn_s_down, dtype=np.float32),
    )
    return out[None, :, :]
